# version 09
# baseline (speedup 1.0000x reference)
import numpy as np
import ml_dtypes
import concourse.bass as bass
import concourse.mybir as mybir
from concourse.bass_utils import run_bass_kernel_spmd
from contextlib import ExitStack

F32 = mybir.dt.float32
BF16 = mybir.dt.bfloat16
I32 = mybir.dt.int32
AF = mybir.ActivationFunctionType
ALU = mybir.AluOpType
AX = mybir.AxisListType

D = 4096
KB = 32
TC = 1024
INTER = 11008
TAU = 86
NCORE = 8
EPS = 1e-5
NZCH = 11
DN_GROUPS = [3] * 10 + [2]
NRES = 36


class SemT:
    def __init__(self, h):
        self.h = h
        self.n = 0


def build(nc, dbg=None):
    x_in = nc.dram_tensor("x", [TC, D], F32, kind="ExternalInput").ap()
    w_in = nc.dram_tensor("w", [D, D], F32, kind="ExternalInput").ap()
    m_in = nc.dram_tensor("m", [D, D], I32, kind="ExternalInput").ap()
    qgu_in = nc.dram_tensor("qgu", [TAU, 2, 128, 512], I32, kind="ExternalInput").ap()
    qzgu_in = nc.dram_tensor("qzgu", [32, 2752], I32, kind="ExternalInput").ap()
    sgu_in = nc.dram_tensor("sgu", [32, 2 * INTER], F32, kind="ExternalInput").ap()
    qdn_in = nc.dram_tensor("qdn", [128, TAU * 512], I32, kind="ExternalInput").ap()
    qzdn_in = nc.dram_tensor("qzdn", [TAU, 512], I32, kind="ExternalInput").ap()
    sdn_in = nc.dram_tensor("sdn", [TAU, D], F32, kind="ExternalInput").ap()
    sgur_in = nc.dram_tensor("sgur", [TAU, 8192], F32, kind="ExternalInput").ap()
    sdnr_in = nc.dram_tensor("sdnr", [1, TAU * D], F32, kind="ExternalInput").ap()
    toep_in = nc.dram_tensor("toep", [128, 173], BF16, kind="ExternalInput").ap()
    bsel_in = nc.dram_tensor("bsel", [128, 256], BF16, kind="ExternalInput").ap()
    ones_in = nc.dram_tensor("onesr", [1, 128], BF16, kind="ExternalInput").ap()
    eye_in = nc.dram_tensor("eye", [128, 128], BF16, kind="ExternalInput").ap()
    out_t = nc.dram_tensor("out_t", [D, TC], F32, kind="ExternalOutput").ap()

    xbf_d = nc.dram_tensor("xbf_d", [TC, D], BF16, kind="Internal").ap()
    wbf_d = nc.dram_tensor("wbf_d", [D, D], BF16, kind="Internal").ap()
    b01t_d = nc.dram_tensor("b01t_d", [128, D], BF16, kind="Internal").ap()
    r1_d = nc.dram_tensor("r1_d", [KB, 128, TC], BF16, kind="Internal").ap()
    y1_d = nc.dram_tensor("y1_d", [KB, 128, TC], BF16, kind="Internal").ap()
    h_d = nc.dram_tensor("h_d", [TAU, 128, TC], BF16, kind="Internal").ap()
    r2_d = nc.dram_tensor("r2_d", [KB, 128, TC], BF16, kind="Internal").ap()
    zsgu_d = nc.dram_tensor("zsgu_d", [32, 2 * INTER], BF16, kind="Internal").ap()

    do_x = dbg not in ("mask",)
    do_w = dbg not in ("mask", "xonly")
    do_attn = dbg not in ("mask", "xonly", "wprep")
    NCH = 1 if dbg == "attn1" else 8
    dbg_outs = {}
    if dbg is not None:
        for name, shape in (("d_r1", [KB, 128, TC]), ("d_y1", [KB, 128, TC]),
                            ("d_h", [TAU, 128, TC]), ("d_r2", [KB, 128, TC]),
                            ("d_zsgu", [32, 2 * INTER]), ("d_zsdn", [TAU, D]),
                            ("d_b01t", [128, D]), ("d_cs", [32, TC]),
                            ("d_hs", [TAU, TC])):
            dbg_outs[name] = nc.dram_tensor(name, shape, BF16,
                                            kind="ExternalOutput").ap()

    es = ExitStack()

    def mksem(name):
        return SemT(es.enter_context(nc.semaphore(name)))

    qXc = mksem("qXc"); qXs = mksem("qXs"); qXt = mksem("qXt")
    qMk = mksem("qMk"); qWc = mksem("qWc"); qWs = mksem("qWs")
    qBs = mksem("qBs"); qMt = mksem("qMt"); qWt = mksem("qWt")
    qRs = mksem("qRs"); qRl = mksem("qRl"); qY1 = mksem("qY1")
    qZL = mksem("qZL"); qZS = mksem("qZS"); qYL = mksem("qYL"); qZC = mksem("qZC")
    qQG = mksem("qQG"); qSB = mksem("qSB"); qHs = mksem("qHs")
    qHl = mksem("qHl"); qH2 = mksem("qH2"); qQD = mksem("qQD")
    qSD = mksem("qSD"); qYl = mksem("qYl"); qR2s = mksem("qR2s")
    qR2l = mksem("qR2l"); qOut = mksem("qOut"); qMisc = mksem("qMisc")
    qHlA = mksem("qHlA"); qHlB = mksem("qHlB")
    qHa = mksem("qHa"); qHb = mksem("qHb")
    qYa = mksem("qYa"); qYb = mksem("qYb")
    qRa = mksem("qRa"); qRb = mksem("qRb")
    qRlA = mksem("qRlA"); qRlB = mksem("qRlB")
    qXcA = mksem("qXcA"); qXcB = mksem("qXcB")
    qXsA = mksem("qXsA"); qXsB = mksem("qXsB")
    qWcA = mksem("qWcA"); qWcB = mksem("qWcB")
    qMtA = mksem("qMtA"); qMtB = mksem("qMtB")
    qRsA = mksem("qRsA"); qRsB = mksem("qRsB")
    qZL2 = mksem("qZL2")
    qQGA = mksem("qQGA"); qQGB = mksem("qQGB")
    qSBA = mksem("qSBA"); qSBB = mksem("qSBB")
    qZCA = mksem("qZCA"); qZCB = mksem("qZCB")
    qHsA = mksem("qHsA"); qHsB = mksem("qHsB")
    qSDA = mksem("qSDA"); qSDB = mksem("qSDB")
    qR2sS = [mksem(f"qR2s{j}") for j in range(4)]
    qOutA = mksem("qOutA"); qOutB = mksem("qOutB")
    vMr = mksem("vMr"); vB1 = mksem("vB1"); tB = mksem("tB")
    vWm = mksem("vWm"); tA = mksem("tA"); vAE = mksem("vAE")
    aSq = mksem("aSq"); tS1 = mksem("tS1"); vF1 = mksem("vF1")
    aF1 = mksem("aF1"); tBc = mksem("tBc"); vAp = mksem("vAp")
    tCs = mksem("tCs"); vCs = mksem("vCs"); vZB = mksem("vZB")
    vDQ = mksem("vDQ"); tMM = mksem("tMM"); aSI = mksem("aSI"); vEV = mksem("vEV")
    tHs = mksem("tHs"); vHs = mksem("vHs"); vZD = mksem("vZD")
    vDQ2 = mksem("vDQ2"); tDN = mksem("tDN"); vDE = mksem("vDE")
    tTf = mksem("tTf"); aTE = mksem("aTE")
    aSq2 = mksem("aSq2"); tS2 = mksem("tS2"); vF2 = mksem("vF2")
    aF2 = mksem("aF2"); tBc2 = mksem("tBc2"); vAp2 = mksem("vAp2")
    vSS = mksem("vSS"); qFl = mksem("qFl")

    with es:
        toep = es.enter_context(nc.sbuf_tensor("toep_s", [128, 173], BF16))
        onesr = es.enter_context(nc.sbuf_tensor("onesr_s", [1, 128], BF16))
        hsum = es.enter_context(nc.sbuf_tensor("hsum", [TAU, TC], BF16))
        row_m = es.enter_context(nc.sbuf_tensor("row_m", [1, TC], F32))
        row_a = es.enter_context(nc.sbuf_tensor("row_a", [1, TC], F32))
        row_b = es.enter_context(nc.sbuf_tensor("row_b", [1, TC], F32))
        row_c = es.enter_context(nc.sbuf_tensor("row_c", [1, TC], F32))
        row_rsb = es.enter_context(nc.sbuf_tensor("row_rsb", [1, TC], BF16))
        row_mrb = es.enter_context(nc.sbuf_tensor("row_mrb", [1, TC], BF16))
        fl_s = es.enter_context(nc.sbuf_tensor("fl_s", [1, 32], BF16))
        fl_v = es.enter_context(nc.sbuf_tensor("fl_v", [1, 32], BF16))
        fl_c = es.enter_context(nc.sbuf_tensor("fl_c", [1, 32], BF16))

        def vfl(ve, sem, k=1):
            ve.tensor_copy(fl_v[:], fl_s[:]).then_inc(sem.h, k)

        def afl(ac, sem, k=1):
            ac.activation(fl_c[:], fl_s[:], AF.Copy).then_inc(sem.h, k)
        csum = es.enter_context(nc.sbuf_tensor("csum", [32, TC], BF16))
        zsd = es.enter_context(nc.sbuf_tensor("zsd", [TAU, D], BF16))

        with nc.Block() as block:
            def c_gp(gp):
                gp.dma_start(out=toep[:], in_=toep_in).then_inc(qMisc.h, 16)
                gp.dma_start(out=onesr[:], in_=ones_in).then_inc(qMisc.h, 16)
                gp.dma_start(out=fl_s[:], in_=ones_in[:, 0:32]
                             ).then_inc(qFl.h, 16)
            block.gpsimd(c_gp)
            qMisc.n += 32

            esA = ExitStack()
            bsel = esA.enter_context(nc.sbuf_tensor("bsel_s", [128, 256], BF16))
            eye = esA.enter_context(nc.sbuf_tensor("eye_s", [128, 128], BF16))
            xT = esA.enter_context(nc.sbuf_tensor("xT", [128, KB * TC], BF16,
                                                  is_transpose_dest=True))
            xc = [esA.enter_context(nc.sbuf_tensor(f"xc{i}", [128, D], BF16))
                  for i in range(2)]
            wc = [esA.enter_context(nc.sbuf_tensor(f"wc{i}", [128, D], BF16))
                  for i in range(2)]
            mk = [esA.enter_context(nc.sbuf_tensor(f"mk{i}", [128, D], I32))
                  for i in range(1)]
            s1i = esA.enter_context(nc.sbuf_tensor("s1i", [128, 128], F32))
            s1b = esA.enter_context(nc.sbuf_tensor("s1b", [128, 128], BF16))
            b01 = esA.enter_context(nc.sbuf_tensor("b01", [128, 128], BF16))
            wcm = [esA.enter_context(nc.sbuf_tensor(f"wcm{i}", [128, D], BF16))
                   for i in range(2)]
            mkb = [esA.enter_context(nc.sbuf_tensor(f"mkb{i}", [128, 128], BF16))
                   for i in range(2)]
            wT = [esA.enter_context(nc.sbuf_tensor(f"wT{i}", [128, KB * 256], BF16))
                  for i in range(2)]
            rst = [esA.enter_context(nc.sbuf_tensor(f"rst{i}", [128, TC], BF16))
                   for i in range(2)]
            psA = [esA.enter_context(nc.psum_tensor(f"psA{i}", [128, TC], F32))
                   for i in range(2)]
            psT = [esA.enter_context(nc.psum_tensor(f"psT{i}", [128, 512], F32))
                   for i in range(2)]
            NCH2 = 2 * NCH
            NFILL = 8
            N_WT = (2 * NCH2) if do_attn else (32 if do_w else 0)

            def pa_gp(gp):
                gp.dma_start(out=bsel[:], in_=bsel_in).then_inc(qMisc.h, 16)
                gp.dma_start(out=eye[:], in_=eye_in).then_inc(qMisc.h, 16)
                for i in range(8 if do_x else 0):
                    if i >= 2:
                        gp.wait_ge((qXsA if i % 2 == 0 else qXsB).h,
                                   16 * ((i - 2) // 2 + 1))
                    s = qXcA if i % 2 == 0 else qXcB
                    gp.dma_start(out=xc[i % 2][:],
                                 in_=x_in[128 * i:128 * (i + 1), :]
                                 ).then_inc(s.h, 16)
                    s.n += 16
                if do_x:
                    gp.wait_ge(qXt.h, KB * 16)
                for i in range(N_WT):
                    if i >= 2:
                        gp.wait_ge(vWm.h, i - 1)
                    s = qWcA if i % 2 == 0 else qWcB
                    gp.dma_start(out=wc[i % 2][:],
                                 in_=w_in[128 * i:128 * (i + 1), :]
                                 ).then_inc(s.h, 16)
                    s.n += 16
                    if i == 0:
                        gp.wait_ge(qBs.h, 16)
                    s = qMtA if i % 2 == 0 else qMtB
                    gp.dma_start(out=mkb[i % 2][:],
                                 in_=b01t_d[4 * i:4 * i + 4, 0:128]
                                 .unsqueeze(1).broadcast_to((4, 32, 128))
                                 ).then_inc(s.h, 16)
                    s.n += 16
            block.gpsimd(pa_gp)
            qMisc.n += 32

            def pa_sc(sc):
                for i in range(8 if do_x else 0):
                    sc.wait_ge((qXcA if i % 2 == 0 else qXcB).h,
                               16 * (i // 2 + 1))
                    s = qXsA if i % 2 == 0 else qXsB
                    sc.dma_start(out=xbf_d[128 * i:128 * (i + 1), :],
                                 in_=xc[i % 2][:]).then_inc(s.h, 16)
                    s.n += 16
                if do_x:
                    sc.wait_ge(qXt.h, KB * 16)
                sc.wait_ge(vB1.h, 4)
                sc.dma_start(out=b01t_d[0:128, 0:128], in_=b01[:]
                             ).then_inc(qBs.h, 16)
            block.scalar(pa_sc)
            qBs.n += 16

            def pa_sy(sy):
                if do_x:
                    sy.wait_ge(qXsA.h, qXsA.n)
                    sy.wait_ge(qXsB.h, qXsB.n)
                for b in range(KB if do_x else 0):
                    sy.dma_start_transpose(out=xT[:, TC * b:TC * (b + 1)],
                                           in_=xbf_d[:, 128 * b:128 * (b + 1)]
                                           ).then_inc(qXt.h, 16)
                for i in range(32):
                    if i >= 1:
                        sy.wait_ge(vMr.h, i)
                    sy.dma_start(out=mk[0][:],
                                 in_=m_in[128 * i:128 * (i + 1), :]
                                 ).then_inc(qMk.h, 16)
                for i in range(2 * NCH2 if do_attn else 0):
                    sy.wait_ge(vAE.h, i + 1)
                    s = qRsA if i % 2 == 0 else qRsB
                    sy.dma_start(out=r1_d[i], in_=rst[i % 2][:]
                                 ).then_inc(s.h, 16)
                    s.n += 16
            block.sync(pa_sy)
            qXt.n += KB * 16 if do_x else 0
            qMk.n += 32 * 16

            def pa_ve(ve):
                ve.wait_ge(qFl.h, 16)
                for i in range(32):
                    ve.wait_ge(qMk.h, 16 * (i + 1))
                    src = mk[0][:].rearrange("p (k r) -> p k r", r=32)
                    ve.tensor_reduce(s1i[:], src, AX.X, ALU.add
                                     ).then_inc(vSS.h, 1)
                    vSS.n += 1
                    ve.wait_ge(vSS.h, vSS.n)
                    if i >= 1:
                        ve.wait_ge(tB.h, i)
                    ins_ = ve.tensor_copy(s1b[:], s1i[:])
                    ins_.then_inc(vMr.h, 1)
                    if i % 8 == 7:
                        q = i // 8
                        ve.wait_ge(tB.h, 8 * (q + 1))
                        ins_ = ve.tensor_single_scalar(b01[32 * q:32 * (q + 1), :],
                                                psA[1][0:32, 896:1024],
                                                0.0, ALU.is_gt)
                        ins_.then_inc(vB1.h, 1)
                def emit_tt(w):
                    ve.wait_ge((qWcA if w % 2 == 0 else qWcB).h,
                               16 * (w // 2 + 1))
                    ve.wait_ge((qMtA if w % 2 == 0 else qMtB).h,
                               16 * (w // 2 + 1))
                    if w >= 2:
                        ve.wait_ge(tTf.h, NFILL * (w - 1))
                    ins_ = ve.tensor_mul(
                        wcm[w % 2][:].rearrange("p (k r) -> p k r", r=32),
                        wc[w % 2][:].rearrange("p (k r) -> p k r", r=32),
                        mkb[w % 2][:].unsqueeze(2).broadcast_to((128, 128, 32)))
                    ins_.then_inc(vWm.h, 1)

                if do_attn:
                    for w0 in range(min(6, N_WT)):
                        emit_tt(w0)
                    for j in range(NCH2):
                        ve.wait_ge(tA.h, 32 * (j + 1))
                        for nt in range(2):
                            i = j * 2 + nt
                            if i >= 2:
                                ve.wait_ge((qRsA if i % 2 == 0 else qRsB).h,
                                           16 * ((i - 2) // 2 + 1))
                            ins_ = ve.tensor_add(rst[i % 2][:], psA[nt][:],
                                          xT[:, TC * i:TC * (i + 1)])
                            ins_.then_inc(vAE.h, 1)
                        for w2 in (2 * (j + 3), 2 * (j + 3) + 1):
                            if 6 <= w2 < N_WT:
                                emit_tt(w2)
                else:
                    for w in range(N_WT):
                        emit_tt(w)
            block.vector(pa_ve)
            vMr.n += 32
            vB1.n += 4
            vWm.n += N_WT
            vAE.n += 2 * NCH2 if do_attn else 0

            def pa_ac(ac):
                ac.wait_ge(qFl.h, 16)
                for f in range(NFILL * N_WT):
                    i = f // NFILL
                    fl2_ = f % NFILL
                    j = i // 2
                    t2 = i % 2
                    g4 = fl2_
                    ac.wait_ge(tTf.h, f + 1)
                    if do_attn and j >= 2:
                        ac.wait_ge(tA.h, 32 * (j - 1))
                    dst = wT[j % 2][:, 4096 * t2 + 512 * g4:
                                     4096 * t2 + 512 * (g4 + 1)]
                    ins_ = ac.activation(dst, psT[f % 2][:], AF.Copy)
                    ins_.then_inc(aTE.h, 1)
            block.scalar(pa_ac)
            aTE.n += NFILL * N_WT

            def pa_te(te):
                te.wait_ge(qMisc.h, 64)
                for i in range(32):
                    q, r = divmod(i, 8)
                    te.wait_ge(vMr.h, i + 1)
                    if r == 0 and q > 0:
                        te.wait_ge(vB1.h, q)
                    te.matmul(psA[1][0:32, 896:1024], bsel[:, 32 * r:32 * (r + 1)],
                              s1b[:], start=(r == 0), stop=(r == 7)
                              ).then_inc(tB.h, 1)
                def emit_fills(i):
                    te.wait_ge(vWm.h, i + 1)
                    for g4 in range(NFILL):
                        f = i * NFILL + g4
                        if f >= 2:
                            te.wait_ge(aTE.h, f - 1)
                        ins = None
                        for cc in range(4):
                            c = 4 * g4 + cc
                            ins = te.matmul(psT[f % 2][:, 128 * cc:128 * (cc + 1)],
                                            wcm[i % 2][:, 128 * c:128 * (c + 1)],
                                            eye[:], start=(cc == 0),
                                            stop=(cc == 3), skip_group_check=True)
                        ins.then_inc(tTf.h, 1)
                    te.then_last = None

                for i0 in range(min(4, N_WT)):
                    emit_fills(i0)
                if do_attn:
                    te.wait_ge(qXt.h, KB * 16)
                    te.wait_ge(vB1.h, 4)
                for j in range(NCH2 if do_attn else 0):
                    if j > 0:
                        te.wait_ge(vAE.h, 2 * j)
                    te.wait_ge(aTE.h, NFILL * 2 * (j + 1))
                    for kb in range(KB):
                        ins = None
                        for nt in range(2):
                            for hf in range(2):
                                ins = te.matmul(
                                    psA[nt][:, 512 * hf:512 * (hf + 1)],
                                    wT[j % 2][:, 4096 * nt + 128 * kb:
                                              4096 * nt + 128 * (kb + 1)],
                                    xT[:, TC * kb + 512 * hf:
                                       TC * kb + 512 * (hf + 1)],
                                    start=(kb == 0), stop=(kb == KB - 1))
                        ins.then_inc(tA.h, 1)
                    for i2 in (2 * (j + 2), 2 * (j + 2) + 1):
                        if 4 <= i2 < N_WT:
                            emit_fills(i2)
                if not do_attn and do_w:
                    for i in range(4, N_WT):
                        emit_fills(i)
            block.tensor(pa_te)
            tB.n += 32
            tTf.n += NFILL * N_WT
            tA.n += 32 * NCH2 if do_attn else 0

            if dbg in ("mask", "xonly", "wprep", "attn", "attn1"):
                def dbg_gp(gp):
                    n = 0
                    if do_attn:
                        gp.wait_ge(qRsA.h, qRsA.n)
                        gp.wait_ge(qRsB.h, qRsB.n)
                        gp.dma_start(out=dbg_outs["d_r1"][0:4 * NCH],
                                     in_=r1_d[0:4 * NCH]
                                     ).then_inc(qMisc.h, 16)
                        n += 16
                    gp.wait_ge(qBs.h, qBs.n)
                    gp.dma_start(out=dbg_outs["d_b01t"][:], in_=b01t_d[:]
                                 ).then_inc(qMisc.h, 16)
                    n += 16
                    if do_x:
                        gp.wait_ge(qXt.h, qXt.n)
                        for b in range(KB):
                            gp.dma_start(out=dbg_outs["d_y1"][b],
                                         in_=xT[:, TC * b:TC * (b + 1)]
                                         ).then_inc(qMisc.h, 16)
                        n += 16 * KB
                    if do_w and not do_attn:
                        gp.wait_ge(aTE.h, aTE.n)
                        gp.dma_start(
                            out=dbg_outs["d_r2"][0:8].transpose([1, 0, 2]),
                            in_=wT[1][:, 0:8192].rearrange(
                                "p (b t) -> p b t", b=8)
                            ).then_inc(qMisc.h, 16)
                        n += 16
                    gp.wait_ge(qMisc.h, qMisc.n + n)
                block.gpsimd(dbg_gp)
                esA.close()
                return
            esA.close()

            esB = ExitStack()
            rsl = esB.enter_context(nc.sbuf_tensor("rsl", [128, KB * TC], BF16))
            y1T = esB.enter_context(nc.sbuf_tensor("y1T", [128, KB * TC], BF16))
            rbc = esB.enter_context(nc.sbuf_tensor("rbc", [128, TC], BF16))
            mrbc = esB.enter_context(nc.sbuf_tensor("mrbc", [128, TC], BF16))
            rsq = [esB.enter_context(nc.sbuf_tensor(f"rsq{i}", [128, TC], BF16))
                   for i in range(2)]
            tmpT = esB.enter_context(nc.sbuf_tensor("tmpT", [128, TC], BF16))
            psS = esB.enter_context(nc.psum_tensor("psS", [128, TC], F32))
            psR = esB.enter_context(nc.psum_tensor("psR", [128, TC], F32))
            psM = esB.enter_context(nc.psum_tensor("psM", [128, TC], F32))
            psC = esB.enter_context(nc.psum_tensor("psC", [128, TC], F32))

            def lb_sy(sy):
                sy.wait_ge(qRsA.h, qRsA.n)
                sy.wait_ge(qRsB.h, qRsB.n)
                sy.dma_start(
                    out=rsl[:, 0:16 * TC].rearrange("p (b t) -> p b t", b=16),
                    in_=r1_d[0:16].transpose([1, 0, 2])).then_inc(qRlA.h, 16)
                sy.dma_start(
                    out=rsl[:, 16 * TC:32 * TC].rearrange("p (b t) -> p b t",
                                                          b=16),
                    in_=r1_d[16:32].transpose([1, 0, 2])).then_inc(qRlB.h, 16)
            block.sync(lb_sy)
            qRlA.n += 16
            qRlB.n += 16

            def lb_ac(ac):
                for dt in range(KB):
                    ac.wait_ge((qRlA if dt < 16 else qRlB).h, 16)
                    if dt >= 2:
                        ac.wait_ge(tS1.h, dt - 1)
                    ins_ = ac.activation(rsq[dt % 2][:], rsl[:, TC * dt:TC * (dt + 1)],
                                  AF.Square)
                    ins_.then_inc(aSq.h, 1)
                ac.wait_ge(vF1.h, 4)
                ins_ = ac.activation(row_b[:], row_a[:], AF.Sqrt)
                ins_.then_inc(aF1.h, 1)
                ac.wait_ge(tBc.h, 4)
                ac.activation(rbc[:], psR[:], AF.Copy)
                ins_ = ac.activation(mrbc[:], psM[:], AF.Copy)
                ins_.then_inc(aF1.h, 2)
            block.scalar(lb_ac)
            aSq.n += KB
            aF1.n += 3

            def lb_te(te):
                for dt in range(KB):
                    te.wait_ge(aSq.h, dt + 1)
                    ins = None
                    for hf in range(2):
                        sl = slice(512 * hf, 512 * (hf + 1))
                        ins = te.matmul(
                            psS[0:64, sl], toep[:, 86:150],
                            rsl[:, TC * dt + 512 * hf:TC * dt + 512 * (hf + 1)],
                            start=(dt == 0), stop=False, skip_group_check=True)
                        ins = te.matmul(
                            psS[0:64, sl], toep[:, 54:118], rsq[dt % 2][:, sl],
                            start=False, stop=(dt == KB - 1), skip_group_check=True)
                    ins.then_inc(tS1.h, 1)
                te.wait_ge(vF1.h, 7)
                for hf in range(2):
                    sl = slice(512 * hf, 512 * (hf + 1))
                    te.matmul(psR[:, sl], onesr[:], row_rsb[:, sl], start=True,
                              stop=True).then_inc(tBc.h, 1)
                    te.matmul(psM[:, sl], onesr[:], row_mrb[:, sl], start=True,
                              stop=True).then_inc(tBc.h, 1)
                for dt in range(KB):
                    te.wait_ge(vAp.h, 2 * (dt + 1))
                    ins = None
                    for hf in range(2):
                        ins = te.matmul(
                            psC[0:32, 512 * hf:512 * (hf + 1)],
                            toep[:, 86 - dt:118 - dt],
                            y1T[:, TC * dt + 512 * hf:TC * dt + 512 * (hf + 1)],
                            start=(dt == 0), stop=(dt == KB - 1))
                    ins.then_inc(tCs.h, 1)
            block.tensor(lb_te)
            tS1.n += KB
            tBc.n += 4
            tCs.n += KB

            def lb_ve(ve):
                ve.wait_ge(tS1.h, KB)
                ve.tensor_scalar_mul(row_m[:], psS[0:1, :], 1.0 / D)
                ve.tensor_scalar_mul(row_a[:], psS[32:33, :], 1.0 / D)
                ve.tensor_mul(row_b[:], row_m[:], row_m[:])
                ve.tensor_sub(row_c[:], row_a[:], row_b[:])
                ins_ = ve.tensor_scalar_add(row_a[:], row_c[:], EPS)
                ins_.then_inc(vF1.h, 4)
                ve.wait_ge(aF1.h, 1)
                ve.reciprocal(row_c[:], row_b[:])
                ve.tensor_mul(row_a[:], row_m[:], row_c[:])
                ve.tensor_copy(row_rsb[:], row_c[:])
                ins_ = ve.tensor_copy(row_mrb[:], row_a[:])
                ins_.then_inc(vF1.h, 3)
                ve.wait_ge(aF1.h, 3)
                for dt in range(KB):
                    sl = slice(TC * dt, TC * (dt + 1))
                    ve.tensor_mul(tmpT[:], rsl[:, sl], rbc[:])
                    ins_ = ve.tensor_sub(y1T[:, sl], tmpT[:], mrbc[:])
                    ins_.then_inc(vAp.h, 2)
                ve.wait_ge(tCs.h, KB)
                ins_ = ve.tensor_copy(csum[:], psC[0:32, :])
                ins_.then_inc(vCs.h, 1)
            block.vector(lb_ve)
            vF1.n += 7
            vAp.n += 2 * KB
            vCs.n += 1

            def lb_sc(sc):
                for dt in range(KB):
                    sc.wait_ge(vAp.h, 2 * (dt + 1))
                    sc.dma_start(out=y1_d[dt], in_=y1T[:, TC * dt:TC * (dt + 1)]
                                 ).then_inc(qY1.h, 16)
            block.scalar(lb_sc)
            qY1.n += KB * 16

            if dbg == "ln1":
                def dbg_gp(gp):
                    gp.wait_ge(qY1.h, qY1.n)
                    gp.wait_ge(vCs.h, vCs.n)
                    gp.dma_start(out=dbg_outs["d_y1"][:], in_=y1_d[:]
                                 ).then_inc(qMisc.h, 16)
                    gp.dma_start(out=dbg_outs["d_cs"][:], in_=csum[:]
                                 ).then_inc(qMisc.h, 16)
                    gp.wait_ge(qMisc.h, qMisc.n + 32)
                block.gpsimd(dbg_gp)
                esB.close()
                return
            esB.close()

            esY = ExitStack()
            y1T2 = esY.enter_context(nc.sbuf_tensor("y1T2", [128, KB * TC], BF16))
            esZ = ExitStack()
            qzt = esZ.enter_context(nc.sbuf_tensor("qzt", [32, 688], I32))
            znb = esZ.enter_context(nc.sbuf_tensor("znb", [32, 5504], I32))
            sst = esZ.enter_context(nc.sbuf_tensor("sst", [32, 5504], BF16))
            zst = esZ.enter_context(nc.sbuf_tensor("zst", [32, 5504], BF16))
            qzt2 = esZ.enter_context(nc.sbuf_tensor("qzt2", [TAU, 512], I32))
            znb2 = esZ.enter_context(nc.sbuf_tensor("znb2", [TAU, D], I32))
            sst2 = esZ.enter_context(nc.sbuf_tensor("sst2", [TAU, D], BF16))

            def z_gp(gp):
                gp.wait_ge(qY1.h, qY1.n)
                gp.wait_ge(vCs.h, vCs.n)
                for ch in range(4):
                    if ch >= 1:
                        gp.wait_ge(vZB.h, 9 * ch)
                    gp.dma_start(out=qzt[:],
                                 in_=qzgu_in[:, 688 * ch:688 * (ch + 1)]
                                 ).then_inc(qZL.h, 16)
                    gp.dma_start(out=sst[:],
                                 in_=sgu_in[:, 5504 * ch:5504 * (ch + 1)]
                                 ).then_inc(qZL.h, 16)
                gp.dma_start(out=qzt2[:], in_=qzdn_in).then_inc(qZL2.h, 16)
                gp.dma_start(out=sst2[:], in_=sdn_in).then_inc(qZL2.h, 16)
                for dt in range(KB):
                    gp.dma_start(out=y1T2[:, TC * dt:TC * (dt + 1)], in_=y1_d[dt]
                                 ).then_inc(qYL.h, 16)
            block.gpsimd(z_gp)
            qZL.n += 8 * 16
            qZL2.n += 2 * 16
            qYL.n += KB * 16

            def z_ve(ve):
                for ch in range(4):
                    ve.wait_ge(qZL.h, 32 * (ch + 1))
                    if ch >= 1:
                        ve.wait_ge(qZS.h, 16 * ch)
                    for j in range(8):
                        ve.tensor_scalar(znb[:, 688 * j:688 * (j + 1)], qzt[:],
                                         4 * j, 15, ALU.logical_shift_right,
                                         ALU.bitwise_and).then_inc(vZB.h, 1)
                    nv = znb[:].rearrange("p (j q) -> p q j", j=8)
                    ins_ = ve.scalar_tensor_tensor(zst[:], nv, -1.0, sst[:], ALU.mult,
                                            ALU.mult)
                    ins_.then_inc(vZB.h, 1)
                ve.wait_ge(qZL.h, 128)
                ve.wait_ge(qZL2.h, 32)
                for j in range(8):
                    ve.tensor_scalar(znb2[:, 512 * j:512 * (j + 1)], qzt2[:],
                                     4 * j, 15, ALU.logical_shift_right,
                                     ALU.bitwise_and)
                nv2 = znb2[:].rearrange("p (j q) -> p q j", j=8)
                ins_ = ve.scalar_tensor_tensor(zsd[:], nv2, -1.0, sst2[:], ALU.mult,
                                        ALU.mult)
                ins_.then_inc(vZD.h, 1)
            block.vector(z_ve)
            vZB.n += 36
            vZD.n += 1

            def z_sc(sc):
                for ch in range(4):
                    sc.wait_ge(vZB.h, 9 * (ch + 1))
                    sc.dma_start(out=zsgu_d[:, 5504 * ch:5504 * (ch + 1)],
                                 in_=zst[:]).then_inc(qZS.h, 16)
            block.scalar(z_sc)
            qZS.n += 4 * 16
            esZ.close()

            esC = ExitStack()
            qg = [esC.enter_context(nc.sbuf_tensor(f"qg{i}", [128, 1024], I32))
                  for i in range(2)]
            nib = esC.enter_context(nc.sbuf_tensor("nib", [128, 2048], I32))
            deq = [esC.enter_context(nc.sbuf_tensor(f"deq{i}", [128, 8192], BF16))
                   for i in range(2)]
            sbt = [esC.enter_context(nc.sbuf_tensor(f"sbt{i}", [128, 8192], BF16))
                   for i in range(2)]
            zsg = [esC.enter_context(nc.sbuf_tensor(f"zsg{i}", [32, 1024], BF16))
                   for i in range(2)]
            hst = [esC.enter_context(nc.sbuf_tensor(f"hst{i}", [128, TC], BF16))
                   for i in range(2)]
            hso = [esC.enter_context(nc.sbuf_tensor(f"hso{i}", [128, TC], BF16))
                   for i in range(2)]
            psG = [esC.enter_context(nc.psum_tensor(f"psG{i}", [128, TC], F32))
                   for i in range(2)]
            psU = [esC.enter_context(nc.psum_tensor(f"psU{i}", [128, TC], F32))
                   for i in range(2)]

            def c_gp2(gp):
                gp.wait_ge(qZS.h, qZS.n)
                gp.wait_ge(qYL.h, qYL.n)
                gp.wait_ge(vZD.h, vZD.n)
                NZC4 = (TAU + 3) // 4
                for c in range(2):
                    t0 = 4 * c
                    tn = min(4, TAU - t0)
                    s = qZCA if c % 2 == 0 else qZCB
                    gp.dma_start(out=zsg[c % 2][:, 0:128 * tn],
                                 in_=zsgu_d[:, 128 * t0:128 * (t0 + tn)]
                                 ).then_inc(s.h, 16)
                    gp.dma_start(out=zsg[c % 2][:, 512:512 + 128 * tn],
                                 in_=zsgu_d[:, INTER + 128 * t0:
                                            INTER + 128 * (t0 + tn)]
                                 ).then_inc(s.h, 16)
                    s.n += 32
                for tau in range(TAU):
                    if tau % 4 == 0 and tau >= 4:
                        c = tau // 4 + 1
                        if c < NZC4:
                            t0 = 4 * c
                            tn = min(4, TAU - t0)
                            gp.wait_ge(tMM.h, 4 * (c - 1))
                            s = qZCA if c % 2 == 0 else qZCB
                            gp.dma_start(out=zsg[c % 2][:, 0:128 * tn],
                                         in_=zsgu_d[:, 128 * t0:128 * (t0 + tn)]
                                         ).then_inc(s.h, 16)
                            gp.dma_start(out=zsg[c % 2][:, 512:512 + 128 * tn],
                                         in_=zsgu_d[:, INTER + 128 * t0:
                                                    INTER + 128 * (t0 + tn)]
                                         ).then_inc(s.h, 16)
                            s.n += 32
                    if tau >= 2:
                        gp.wait_ge(vDQ.h, 2 * (tau - 1))
                    s = qQGA if tau % 2 == 0 else qQGB
                    gp.dma_start(out=qg[tau % 2][:],
                                 in_=qgu_in[tau].transpose([1, 0, 2])
                                 ).then_inc(s.h, 16)
                    s.n += 16
                    s = qSBA if tau % 2 == 0 else qSBB
                    for g in range(2):
                        src = sgu_in[:, INTER * g + 128 * tau:
                                     INTER * g + 128 * (tau + 1)]
                        src = src.unsqueeze(0).broadcast_to((128, 32, 128))
                        gp.dma_start(out=sbt[tau % 2][:, 4096 * g:4096 * (g + 1)],
                                     in_=src).then_inc(s.h, 16)
                        s.n += 16
            block.gpsimd(c_gp2)

            def c_ve(ve):
                for tau in range(TAU):
                    par = tau % 2
                    if tau >= 2:
                        t2 = tau - 2
                        ve.wait_ge(aSI.h, t2 + 1)
                        if t2 >= 2:
                            ve.wait_ge((qHsA if t2 % 2 == 0 else qHsB).h,
                                       16 * ((t2 - 2) // 2 + 1))
                        ins_ = ve.tensor_mul(hso[t2 % 2][:], hst[t2 % 2][:],
                                      psU[t2 % 2][:])
                        ins_.then_inc(vEV.h, 1)
                    ve.wait_ge((qQGA if tau % 2 == 0 else qQGB).h,
                               16 * (tau // 2 + 1))
                    ve.wait_ge((qSBA if tau % 2 == 0 else qSBB).h,
                               32 * (tau // 2 + 1))
                    if tau >= 2:
                        ve.wait_ge(tMM.h, tau - 1)
                    for g in range(2):
                        for sub in range(2):
                            off = 512 * g + 256 * sub
                            for j in range(8):
                                ve.tensor_scalar(nib[:, 256 * j:256 * (j + 1)],
                                                 qg[par][:, off:off + 256],
                                                 4 * j, 15,
                                                 ALU.logical_shift_right,
                                                 ALU.bitwise_and)
                            nv = nib[:].rearrange("p (b kb a) -> p kb a b",
                                                  b=8, kb=16)
                            ins = ve.tensor_mul(
                                deq[par][:, 4096 * g + 2048 * sub:
                                         4096 * g + 2048 * (sub + 1)],
                                nv,
                                sbt[par][:, 4096 * g + 2048 * sub:
                                         4096 * g + 2048 * (sub + 1)])
                        ins.then_inc(vDQ.h, 1)
                for t2 in (TAU - 2, TAU - 1):
                    ve.wait_ge(aSI.h, t2 + 1)
                    ve.wait_ge((qHsA if t2 % 2 == 0 else qHsB).h,
                               16 * ((t2 - 2) // 2 + 1))
                    ins_ = ve.tensor_mul(hso[t2 % 2][:], hst[t2 % 2][:],
                                  psU[t2 % 2][:])
                    ins_.then_inc(vEV.h, 1)
            block.vector(c_ve)
            vDQ.n += 2 * TAU
            vEV.n += TAU

            def c_ac(ac):
                for tau in range(TAU):
                    ac.wait_ge(tMM.h, tau + 1)
                    if tau >= 2:
                        ac.wait_ge(vEV.h, tau - 1)
                    ins_ = ac.activation(hst[tau % 2][:], psG[tau % 2][:], AF.Silu)
                    ins_.then_inc(aSI.h, 1)
                    if tau >= 2:
                        t2 = tau - 2
                        ac.wait_ge(vEV.h, t2 + 1)
                        s = qHsA if t2 % 2 == 0 else qHsB
                        ac.dma_start(out=h_d[t2], in_=hso[t2 % 2][:]
                                     ).then_inc(s.h, 16)
                        s.n += 16
                for t2 in (TAU - 2, TAU - 1):
                    ac.wait_ge(vEV.h, t2 + 1)
                    s = qHsA if t2 % 2 == 0 else qHsB
                    ac.dma_start(out=h_d[t2], in_=hso[t2 % 2][:]
                                 ).then_inc(s.h, 16)
                    s.n += 16
            block.scalar(c_ac)
            aSI.n += TAU

            def c_te(te):
                te.wait_ge(vCs.h, vCs.n)
                for tau in range(TAU):
                    par = tau % 2
                    zch = tau // 4
                    zoff = 128 * (tau - 4 * zch)
                    te.wait_ge(vDQ.h, 2 * (tau + 1))
                    te.wait_ge((qZCA if zch % 2 == 0 else qZCB).h,
                               32 * (zch // 2 + 1))
                    if tau >= 2:
                        te.wait_ge(aSI.h, tau - 1)
                        te.wait_ge(vEV.h, tau - 1)
                    for g in range(2):
                        ps = psG[par] if g == 0 else psU[par]
                        ins = None
                        for hf in range(2):
                            sl = slice(512 * hf, 512 * (hf + 1))
                            for kb in range(KB):
                                ins = te.matmul(
                                    ps[:, sl],
                                    deq[par][:, 4096 * g + 128 * kb:
                                             4096 * g + 128 * (kb + 1)],
                                    y1T2[:, TC * kb + 512 * hf:
                                         TC * kb + 512 * (hf + 1)],
                                    start=(kb == 0), stop=False)
                            ins = te.matmul(
                                ps[:, sl],
                                zsg[zch % 2][:, 512 * g + zoff:
                                             512 * g + zoff + 128],
                                csum[:, sl], start=False, stop=True,
                                skip_group_check=True)
                        if g == 1:
                            ins.then_inc(tMM.h, 1)
            block.tensor(c_te)
            tMM.n += TAU

            if dbg == "gu":
                def dbg_gp(gp):
                    gp.wait_ge(qHsA.h, qHsA.n)
                    gp.wait_ge(qHsB.h, qHsB.n)
                    gp.dma_start(out=dbg_outs["d_h"][:], in_=h_d[:]
                                 ).then_inc(qMisc.h, 16)
                    gp.dma_start(out=dbg_outs["d_zsgu"][:], in_=zsgu_d[:]
                                 ).then_inc(qMisc.h, 16)
                    gp.wait_ge(qMisc.h, qMisc.n + 32)
                block.gpsimd(dbg_gp)
                esC.close()
                esY.close()
                return
            esC.close()
            esY.close()

            if dbg == "hsum":
                NGRUN = 0
            elif dbg is not None and dbg.startswith("dng"):
                NGRUN = int(dbg[3:])
            else:
                NGRUN = len(DN_GROUPS)
            DNG = DN_GROUPS[:NGRUN]
            esD = ExitStack()
            hsl = esD.enter_context(nc.sbuf_tensor("hsl", [128, NRES * TC], BF16))
            hs2 = [esD.enter_context(nc.sbuf_tensor(f"hs2{i}", [128, 8 * TC], BF16))
                   for i in range(2)]
            qds = esD.enter_context(nc.sbuf_tensor("qds", [128, TAU * 48], I32))
            nib2 = esD.enter_context(nc.sbuf_tensor("nib2", [128, 8 * 8 * 48], I32))
            dq2 = [esD.enter_context(nc.sbuf_tensor(f"dq2{i}", [128, 8 * 384], BF16))
                   for i in range(2)]
            sdt = [esD.enter_context(nc.sbuf_tensor(f"sdt{i}", [128, 8 * 384], BF16))
                   for i in range(2)]
            y1s = [esD.enter_context(nc.sbuf_tensor(f"y1s{i}", [128, TC], BF16))
                   for i in range(2)]
            r2s = [esD.enter_context(nc.sbuf_tensor(f"r2s{i}", [128, TC], BF16))
                   for i in range(4)]
            rsq2 = [esD.enter_context(nc.sbuf_tensor(f"rsq2{i}", [128, TC], BF16))
                    for i in range(2)]
            psD3 = [esD.enter_context(nc.psum_tensor(f"psD{i}", [128, TC], F32))
                    for i in range(3)]
            psS2 = esD.enter_context(nc.psum_tensor("psS2", [128, TC], F32))
            NSCH = (TAU - NRES + 7) // 8
            NG = len(DNG)
            evac_seq = [(gi, hl) for gi, nh in enumerate(DNG)
                        for hl in range(nh)]
            NEV = len(evac_seq)
            evac_cum = [0]
            for gi, nh in enumerate(DNG):
                evac_cum.append(evac_cum[-1] + nh)

            sdn_offs = {}
            _off = 0
            for _gi, _nh in enumerate(DN_GROUPS):
                _ewg = 8 * 16 * _nh
                for _c in range(NZCH):
                    _tn = min(8, TAU - 8 * _c)
                    sdn_offs[(_gi, _c)] = _off
                    _off += _tn * _ewg

            def d_gp(gp):
                gp.wait_ge(qHsA.h, qHsA.n)
                gp.wait_ge(qHsB.h, qHsB.n)
                for gi, nh in enumerate(DNG):
                    wg = 16 * nh
                    w0 = 48 * gi
                    for c in range(NZCH):
                        t0 = 8 * c
                        tn = min(8, TAU - t0)
                        idx = gi * NZCH + c
                        if idx >= 2:
                            gp.wait_ge(vDQ2.h, idx - 1)
                        off = sdn_offs[(gi, c)]
                        s = qSDA if idx % 2 == 0 else qSDB
                        src = sdnr_in[0, off:off + tn * 8 * wg]
                        src = src.unsqueeze(0).broadcast_to((128, tn * 8 * wg))
                        gp.dma_start(out=sdt[idx % 2][:, 0:tn * 8 * wg],
                                     in_=src).then_inc(s.h, 16)
                        s.n += 16
            block.gpsimd(d_gp)

            hsum_tgt = {}
            chunk_tgt = {}
            y1s_tgt = {}

            def d_sy(sy):
                sy.wait_ge(qHsA.h, qHsA.n)
                sy.wait_ge(qHsB.h, qHsB.n)
                sy.dma_start(
                    out=hsl[:, 0:18 * TC].rearrange("p (b t) -> p b t", b=18),
                    in_=h_d[0:18].transpose([1, 0, 2])).then_inc(qHlA.h, 16)
                sy.dma_start(
                    out=hsl[:, 18 * TC:36 * TC].rearrange("p (b t) -> p b t",
                                                          b=18),
                    in_=h_d[18:36].transpose([1, 0, 2])).then_inc(qHlB.h, 16)
                for tau in range(NRES, TAU):
                    if tau >= NRES + 2:
                        sy.wait_ge(tHs.h, tau - 1)
                    s = qHa if tau % 2 == 0 else qHb
                    sy.dma_start(out=hs2[tau % 2][:, 0:TC], in_=h_d[tau]
                                 ).then_inc(s.h, 16)
                    s.n += 16
                    hsum_tgt[tau] = (s, s.n)
                for gi, nh in enumerate(DNG):
                    wg = 16 * nh
                    w0g = TAU * 48 * gi
                    if gi >= 1:
                        sy.wait_ge(vDQ2.h, NZCH * gi)
                    sy.dma_start(out=qds[:, 0:TAU * wg],
                                 in_=qdn_in[:, w0g:w0g + TAU * wg]
                                 ).then_inc(qQD.h, 16)
                    for cp in range(NSCH):
                        t0 = NRES + 8 * cp
                        tn = min(8, TAU - t0)
                        if gi == 0:
                            sy.wait_ge(tHs.h, TAU)
                        if cp >= 2:
                            cons = 6 + (cp - 2) if cp - 2 <= 4 else NZCH
                            sy.wait_ge(tDN.h, NZCH * gi + cons)
                        elif gi >= 1:
                            sy.wait_ge(tDN.h, NZCH * gi)
                        s = qHa if cp % 2 == 0 else qHb
                        sy.dma_start(out=hs2[cp % 2][:, 0:tn * TC],
                                     in_=h_d[t0:t0 + tn].transpose([1, 0, 2])
                                     ).then_inc(s.h, 16)
                        s.n += 16
                        chunk_tgt[(gi, cp)] = (s, s.n)
                    for hl in range(nh):
                        k = evac_cum[gi] + hl
                        if k >= 2:
                            sy.wait_ge(vDE.h, k - 1)
                        ht = 3 * gi + hl
                        s = qYa if k % 2 == 0 else qYb
                        sy.dma_start(out=y1s[k % 2][:], in_=y1_d[ht]
                                     ).then_inc(s.h, 16)
                        s.n += 16
                        y1s_tgt[k] = (s, s.n)
            block.sync(d_sy)
            qHlA.n += 16
            qHlB.n += 16
            qQD.n += NG * 16

            def d_ve(ve):
                ve.wait_ge(tHs.h, TAU)
                ins_ = ve.tensor_copy(hsum[:], psS2[0:TAU, :])
                ins_.then_inc(vHs.h, 1)
                for gi, nh in enumerate(DNG):
                    wg = 16 * nh
                    ve.wait_ge(qQD.h, 16 * (gi + 1))
                    for c in range(NZCH):
                        t0 = 8 * c
                        tn = min(8, TAU - t0)
                        idx = gi * NZCH + c
                        ve.wait_ge((qSDA if idx % 2 == 0 else qSDB).h,
                                   16 * (idx // 2 + 1))
                        if c >= 2:
                            ve.wait_ge(tDN.h, NZCH * gi + c - 1)
                        elif gi >= 1:
                            ve.wait_ge(tDN.h, NZCH * gi - c)
                        for j in range(8):
                            ov = nib2[:, 0:tn * 8 * wg].rearrange(
                                "p (t j w) -> p t j w", t=tn, j=8)[:, :, j, :]
                            iv = qds[:, wg * t0:wg * (t0 + tn)].rearrange(
                                "p (t w) -> p t w", t=tn)
                            ve.tensor_scalar(ov, iv, 4 * j, 15,
                                             ALU.logical_shift_right,
                                             ALU.bitwise_and)
                        for tl in range(tn):
                            nv2 = nib2[:, 0:tn * 8 * wg].rearrange(
                                "p (t j w) -> p t w j", t=tn, j=8)[:, tl]
                            sv = sdt[idx % 2][:, 8 * wg * tl:8 * wg * (tl + 1)]
                            ins_ = ve.tensor_mul(dq2[c % 2][:, 8 * wg * tl:
                                                           8 * wg * (tl + 1)],
                                                 nv2, sv)
                        ins_.then_inc(vDQ2.h, 1)
                    ve.wait_ge(tDN.h, NZCH * (gi + 1))
                    for hl in range(nh):
                        k = evac_cum[gi] + hl
                        ys, yc = y1s_tgt[k]
                        ve.wait_ge(ys.h, yc)
                        if k >= 4:
                            ve.wait_ge(qR2sS[k % 4].h, 16 * ((k - 4) // 4 + 1))
                        ins_ = ve.tensor_add(r2s[k % 4][:], psD3[hl][:], y1s[k % 2][:])
                        ins_.then_inc(vDE.h, 1)
            block.vector(d_ve)
            vHs.n += 1
            vDQ2.n += NG * NZCH
            vDE.n += NEV

            def d_te(te):
                te.wait_ge(vZD.h, vZD.n)
                for tau in range(TAU):
                    if tau == 0:
                        te.wait_ge(qHlA.h, 16)
                    elif tau == 18:
                        te.wait_ge(qHlB.h, 16)
                    elif tau >= NRES:
                        hs_, hc_ = hsum_tgt[tau]
                        te.wait_ge(hs_.h, hc_)
                    src = (hsl[:, TC * tau:TC * (tau + 1)] if tau < NRES
                           else hs2[tau % 2][:, 0:TC])
                    ins = None
                    for hf in range(2):
                        ins = te.matmul(psS2[0:TAU, 512 * hf:512 * (hf + 1)],
                                        toep[:, 86 - tau:172 - tau],
                                        src[:, 512 * hf:512 * (hf + 1)],
                                        start=(tau == 0), stop=(tau == TAU - 1))
                    ins.then_inc(tHs.h, 1)
                te.wait_ge(vHs.h, vHs.n)
                stats_k = 0

                def emit_stats_upto(upto):
                    nonlocal stats_k
                    while stats_k < upto:
                        k = stats_k
                        te.wait_ge(aSq2.h, k + 1)
                        ins = None
                        for hf in range(2):
                            sl = slice(512 * hf, 512 * (hf + 1))
                            te.matmul(psS2[0:64, sl], toep[:, 86:150],
                                      r2s[k % 4][:, sl], start=(k == 0),
                                      stop=False, skip_group_check=True)
                            ins = te.matmul(psS2[0:64, sl], toep[:, 54:118],
                                            rsq2[k % 2][:, sl], start=False,
                                            stop=(k == NEV - 1),
                                            skip_group_check=True)
                        ins.then_inc(tS2.h, 1)
                        stats_k += 1

                for gi, nh in enumerate(DNG):
                    wg = 16 * nh
                    if gi >= 1:
                        te.wait_ge(vDE.h, evac_cum[gi])
                    for tau in range(TAU):
                        c = tau // 8
                        if tau % 8 == 0:
                            te.wait_ge(vDQ2.h, NZCH * gi + c + 1)
                        if tau >= NRES and (tau - NRES) % 8 == 0:
                            cp = (tau - NRES) // 8
                            ps_, pc_ = chunk_tgt[(gi, cp)]
                            te.wait_ge(ps_.h, pc_)
                        tl = tau - 8 * c
                        if tau < NRES:
                            mv = hsl[:, TC * tau:TC * (tau + 1)]
                        else:
                            tl2 = (tau - NRES) % 8
                            mv = hs2[((tau - NRES) // 8) % 2][
                                :, TC * tl2:TC * (tl2 + 1)]
                        ins = None
                        for hl in range(nh):
                            for hf in range(2):
                                ins = te.matmul(
                                    psD3[hl][:, 512 * hf:512 * (hf + 1)],
                                    dq2[c % 2][:, 8 * wg * tl + 128 * hl:
                                               8 * wg * tl + 128 * (hl + 1)],
                                    mv[:, 512 * hf:512 * (hf + 1)],
                                    start=(tau == 0), stop=False)
                        if tau % 8 == 7 or tau == TAU - 1:
                            if tau == TAU - 1:
                                for hl in range(nh):
                                    ht = 3 * gi + hl
                                    for hf in range(2):
                                        ins = te.matmul(
                                            psD3[hl][:, 512 * hf:512 * (hf + 1)],
                                            zsd[:, 128 * ht:128 * (ht + 1)],
                                            hsum[:, 512 * hf:512 * (hf + 1)],
                                            start=False, stop=True,
                                            skip_group_check=True)
                            ins.then_inc(tDN.h, 1)
                    if gi >= 1:
                        emit_stats_upto(evac_cum[gi])
                emit_stats_upto(NEV)
            block.tensor(d_te)
            tHs.n += TAU
            tDN.n += NG * NZCH
            tS2.n += NEV

            def d_ac(ac):
                for k in range(NEV):
                    ac.wait_ge(vDE.h, k + 1)
                    if k >= 2:
                        ac.wait_ge(tS2.h, k - 1)
                    ins_ = ac.activation(rsq2[k % 2][:], r2s[k % 4][:], AF.Square)
                    ins_.then_inc(aSq2.h, 1)
                    if k >= 2:
                        k2 = k - 2
                        gi2, hl2 = evac_seq[k2]
                        ht = 3 * gi2 + hl2
                        ac.wait_ge(tS2.h, k2 + 1)
                        s = qR2sS[k2 % 4]
                        ac.dma_start(out=r2_d[ht], in_=r2s[k2 % 4][:]
                                     ).then_inc(s.h, 16)
                        s.n += 16
                for k2 in range(max(0, NEV - 2), NEV):
                    gi2, hl2 = evac_seq[k2]
                    ht = 3 * gi2 + hl2
                    ac.wait_ge(tS2.h, k2 + 1)
                    s = qR2sS[k2 % 4]
                    ac.dma_start(out=r2_d[ht], in_=r2s[k2 % 4][:]
                                 ).then_inc(s.h, 16)
                    s.n += 16
            block.scalar(d_ac)
            aSq2.n += NEV


            def d_fin_ve(ve):
                ve.wait_ge(tS2.h, NEV)
                ve.tensor_scalar_mul(row_m[:], psS2[0:1, :], 1.0 / D)
                ve.tensor_scalar_mul(row_a[:], psS2[32:33, :], 1.0 / D)
                ve.tensor_mul(row_b[:], row_m[:], row_m[:])
                ve.tensor_sub(row_c[:], row_a[:], row_b[:])
                ins_ = ve.tensor_scalar_add(row_a[:], row_c[:], EPS)
                ins_.then_inc(vF2.h, 4)
                ve.wait_ge(aF2.h, 1)
                ve.reciprocal(row_c[:], row_b[:])
                ve.tensor_mul(row_a[:], row_m[:], row_c[:])
                ve.tensor_copy(row_rsb[:], row_c[:])
                ins_ = ve.tensor_copy(row_mrb[:], row_a[:])
                ins_.then_inc(vF2.h, 3)
            block.vector(d_fin_ve)
            vF2.n += 7

            def d_fin_ac(ac):
                ac.wait_ge(vF2.h, 4)
                ins_ = ac.activation(row_b[:], row_a[:], AF.Sqrt)
                ins_.then_inc(aF2.h, 1)
            block.scalar(d_fin_ac)
            aF2.n += 1

            if dbg is not None:
                def dbg_gp(gp):
                    for s in qR2sS:
                        gp.wait_ge(s.h, s.n)
                    gp.wait_ge(vHs.h, vHs.n)
                    gp.wait_ge(vZD.h, vZD.n)
                    n = 48
                    gp.dma_start(out=dbg_outs["d_r2"][:], in_=r2_d[:]
                                 ).then_inc(qMisc.h, 16)
                    gp.dma_start(out=dbg_outs["d_hs"][:], in_=hsum[:]
                                 ).then_inc(qMisc.h, 16)
                    gp.dma_start(out=dbg_outs["d_zsdn"][:], in_=zsd[:]
                                 ).then_inc(qMisc.h, 16)
                    if dbg == "hsum":
                        gp.dma_start(
                            out=dbg_outs["d_y1"][:].transpose([1, 0, 2]),
                            in_=hsl[:, 0:32 * TC].rearrange(
                                "p (b t) -> p b t", b=32)
                            ).then_inc(qMisc.h, 16)
                        gp.dma_start(
                            out=dbg_outs["d_r1"][0:4].transpose([1, 0, 2]),
                            in_=hsl[:, 32 * TC:36 * TC].rearrange(
                                "p (b t) -> p b t", b=4)
                            ).then_inc(qMisc.h, 16)
                        n += 32
                    gp.wait_ge(qMisc.h, qMisc.n + n)
                block.gpsimd(dbg_gp)
                esD.close()
                return
            esD.close()

            esE = ExitStack()
            r2l = [esE.enter_context(nc.sbuf_tensor(f"r2l{i}", [128, TC], BF16))
                   for i in range(2)]
            otile = [esE.enter_context(nc.sbuf_tensor(f"ot{i}", [128, TC], F32))
                     for i in range(2)]
            tmp2 = esE.enter_context(nc.sbuf_tensor("tmp2", [128, TC], BF16))
            rbc2 = esE.enter_context(nc.sbuf_tensor("rbc2", [128, TC], BF16))
            mrbc2 = esE.enter_context(nc.sbuf_tensor("mrbc2", [128, TC], BF16))
            psR2 = esE.enter_context(nc.psum_tensor("psR2", [128, TC], F32))
            psM2 = esE.enter_context(nc.psum_tensor("psM2", [128, TC], F32))

            def e_te(te):
                te.wait_ge(vF2.h, vF2.n)
                for hf in range(2):
                    sl = slice(512 * hf, 512 * (hf + 1))
                    te.matmul(psR2[:, sl], onesr[:], row_rsb[:, sl], start=True,
                              stop=True).then_inc(tBc2.h, 1)
                    te.matmul(psM2[:, sl], onesr[:], row_mrb[:, sl], start=True,
                              stop=True).then_inc(tBc2.h, 1)
            block.tensor(e_te)
            tBc2.n += 4

            def e_ac(ac):
                ac.wait_ge(tBc2.h, 4)
                ac.activation(rbc2[:], psR2[:], AF.Copy)
                ins_ = ac.activation(mrbc2[:], psM2[:], AF.Copy)
                ins_.then_inc(aF2.h, 2)
            block.scalar(e_ac)
            aF2.n += 2

            r2l_tgt = {}

            def e_sy(sy):
                for s in qR2sS:
                    sy.wait_ge(s.h, s.n)
                for dt in range(KB):
                    if dt >= 2:
                        sy.wait_ge(vAp2.h, 2 * (dt - 1))
                    s = qRa if dt % 2 == 0 else qRb
                    sy.dma_start(out=r2l[dt % 2][:], in_=r2_d[dt]
                                 ).then_inc(s.h, 16)
                    s.n += 16
                    r2l_tgt[dt] = (s, s.n)
            block.sync(e_sy)

            def e_ve(ve):
                ve.wait_ge(aF2.h, aF2.n)
                for dt in range(KB):
                    rs_, rc_ = r2l_tgt[dt]
                    ve.wait_ge(rs_.h, rc_)
                    if dt >= 2:
                        ve.wait_ge((qOutA if dt % 2 == 0 else qOutB).h,
                                   16 * ((dt - 2) // 2 + 1))
                    ve.tensor_mul(tmp2[:], r2l[dt % 2][:], rbc2[:])
                    ins_ = ve.tensor_sub(otile[dt % 2][:], tmp2[:], mrbc2[:])
                    ins_.then_inc(vAp2.h, 2)
            block.vector(e_ve)
            vAp2.n += 2 * KB

            def e_sc(sc):
                for dt in range(KB):
                    sc.wait_ge(vAp2.h, 2 * (dt + 1))
                    s = qOutA if dt % 2 == 0 else qOutB
                    sc.dma_start(out=out_t[128 * dt:128 * (dt + 1), :],
                                 in_=otile[dt % 2][:]).then_inc(s.h, 16)
                    s.n += 16
            block.scalar(e_sc)

            def e_gp(gp):
                gp.wait_ge(qOutA.h, qOutA.n)
                gp.wait_ge(qOutB.h, qOutB.n)
            block.gpsimd(e_gp)
            esE.close()
    return nc



def _consts():
    toep = np.zeros((128, 173), dtype=ml_dtypes.bfloat16)
    toep[:, 86] = 1.0
    bsel = np.zeros((128, 8, 32), dtype=ml_dtypes.bfloat16)
    for r in range(8):
        for p in range(128):
            bsel[p, r, 4 * r + p // 32] = 1.0
    bsel = bsel.reshape(128, 256)
    onesr = np.ones((1, 128), dtype=ml_dtypes.bfloat16)
    eye = np.eye(128, dtype=np.float32).astype(ml_dtypes.bfloat16)
    return toep, bsel, onesr, eye


def _prep_inputs(L0_attn_data, L0_attn_weight, L0_attn_mask, L0_ffn_qweight_gu,
                 L0_ffn_qzeros_gu, L0_ffn_scales_gu, L0_ffn_qweight_dn,
                 L0_ffn_qzeros_dn, L0_ffn_scales_dn):
    x = np.ascontiguousarray(
        np.asarray(L0_attn_data, dtype=np.float32).reshape(8192, D))
    w = np.ascontiguousarray(np.asarray(L0_attn_weight, dtype=np.float32))
    m = np.ascontiguousarray(np.asarray(L0_attn_mask, dtype=np.int32))
    qgu = np.asarray(L0_ffn_qweight_gu, dtype=np.int32)
    qgu_r = np.ascontiguousarray(
        qgu.reshape(32, 128, 2, 86, 16).transpose(3, 2, 1, 0, 4
                                                  ).reshape(86, 2, 128, 512))
    qzgu = np.ascontiguousarray(np.asarray(L0_ffn_qzeros_gu, dtype=np.int32))
    sgu = np.ascontiguousarray(np.asarray(L0_ffn_scales_gu, dtype=np.float32))
    qdn = np.asarray(L0_ffn_qweight_dn, dtype=np.int32)
    A = qdn.reshape(86, 128, 512).transpose(1, 0, 2)
    chunks = [A[:, :, 48 * g:48 * g + 48] for g in range(10)] + [A[:, :, 480:512]]
    qdn_r = np.ascontiguousarray(
        np.concatenate([c.reshape(128, -1) for c in chunks], axis=1))
    qzdn = np.ascontiguousarray(np.asarray(L0_ffn_qzeros_dn, dtype=np.int32))
    sdn = np.ascontiguousarray(np.asarray(L0_ffn_scales_dn, dtype=np.float32))
    sgu_r = np.ascontiguousarray(
        sgu.reshape(32, 2, TAU, 128).transpose(2, 1, 0, 3).reshape(TAU, 8192))
    parts = []
    for gi, nh in enumerate(DN_GROUPS):
        ewg = 8 * 16 * nh
        col0 = 384 * gi
        for c in range(NZCH):
            t0 = 8 * c
            tn = min(8, TAU - t0)
            parts.append(sdn[t0:t0 + tn, col0:col0 + ewg].reshape(-1))
    sdn_r = np.ascontiguousarray(np.concatenate(parts))[None, :]
    toep, bselc, onesr, eye = _consts()
    shared = dict(w=w, m=m, qgu=qgu_r, qzgu=qzgu, sgu=sgu, qdn=qdn_r,
                  qzdn=qzdn, sdn=sdn, sgur=sgu_r, sdnr=sdn_r,
                  toep=np.asarray(toep),
                  bsel=np.asarray(bselc), onesr=np.asarray(onesr),
                  eye=np.asarray(eye))
    in_maps = []
    for c in range(NCORE):
        im = dict(shared)
        im["x"] = np.ascontiguousarray(x[TC * c:TC * (c + 1)])
        in_maps.append(im)
    return in_maps


def kernel(**inputs):
    in_maps = _prep_inputs(**inputs)
    nc = bass.Bass("TRN2", target_bir_lowering=False)
    build(nc)
    res = run_bass_kernel_spmd(nc, in_maps, core_ids=list(range(NCORE)))
    out = np.concatenate([r["out_t"] for r in res.results], axis=1)
    return np.ascontiguousarray(out.T).reshape(2, 4096, 4096).astype(np.float32)


# version 12 (final)
# speedup vs baseline: 1.1379x; 1.1379x over previous
import numpy as np
import ml_dtypes
import concourse.bass as bass
import concourse.mybir as mybir
from concourse.bass_utils import run_bass_kernel_spmd
from contextlib import ExitStack

F32 = mybir.dt.float32
BF16 = mybir.dt.bfloat16
I32 = mybir.dt.int32
AF = mybir.ActivationFunctionType
ALU = mybir.AluOpType
AX = mybir.AxisListType

D = 4096
KB = 32
TC = 1024
INTER = 11008
TAU = 86
NCORE = 8
EPS = 1e-5
NZCH = 11
DN_GROUPS = [3] * 10 + [2]
NRES = 36


class SemT:
    def __init__(self, h):
        self.h = h
        self.n = 0


def build(nc, dbg=None):
    x_in = nc.dram_tensor("x", [TC, D], F32, kind="ExternalInput").ap()
    w_in = nc.dram_tensor("w", [D, D], F32, kind="ExternalInput").ap()
    m_in = nc.dram_tensor("m", [D, D], I32, kind="ExternalInput").ap()
    qgu_in = nc.dram_tensor("qgu", [TAU, 2, 128, 512], I32, kind="ExternalInput").ap()
    qzgu_in = nc.dram_tensor("qzgu", [32, 2752], I32, kind="ExternalInput").ap()
    sgu_in = nc.dram_tensor("sgu", [32, 2 * INTER], F32, kind="ExternalInput").ap()
    qdn_in = nc.dram_tensor("qdn", [128, TAU * 512], I32, kind="ExternalInput").ap()
    qzdn_in = nc.dram_tensor("qzdn", [TAU, 512], I32, kind="ExternalInput").ap()
    sdn_in = nc.dram_tensor("sdn", [TAU, D], F32, kind="ExternalInput").ap()
    sgur_in = nc.dram_tensor("sgur", [TAU, 8192], F32, kind="ExternalInput").ap()
    sdnr_in = nc.dram_tensor("sdnr", [1, TAU * D], F32, kind="ExternalInput").ap()
    toep_in = nc.dram_tensor("toep", [128, 173], BF16, kind="ExternalInput").ap()
    bsel_in = nc.dram_tensor("bsel", [128, 256], BF16, kind="ExternalInput").ap()
    ones_in = nc.dram_tensor("onesr", [1, 128], BF16, kind="ExternalInput").ap()
    eye_in = nc.dram_tensor("eye", [128, 128], BF16, kind="ExternalInput").ap()
    out_t = nc.dram_tensor("out_t", [D, TC], F32, kind="ExternalOutput").ap()

    xbf_d = nc.dram_tensor("xbf_d", [TC, D], BF16, kind="Internal").ap()
    wbf_d = nc.dram_tensor("wbf_d", [D, D], BF16, kind="Internal").ap()
    b01t_d = nc.dram_tensor("b01t_d", [128, D], BF16, kind="Internal").ap()
    r1_d = nc.dram_tensor("r1_d", [KB, 128, TC], BF16, kind="Internal").ap()
    y1_d = nc.dram_tensor("y1_d", [KB, 128, TC], BF16, kind="Internal").ap()
    h_d = nc.dram_tensor("h_d", [TAU, 128, TC], BF16, kind="Internal").ap()
    r2_d = nc.dram_tensor("r2_d", [KB, 128, TC], BF16, kind="Internal").ap()
    zsgu_d = nc.dram_tensor("zsgu_d", [32, 2 * INTER], BF16, kind="Internal").ap()

    do_x = dbg not in ("mask",)
    do_w = dbg not in ("mask", "xonly")
    do_attn = dbg not in ("mask", "xonly", "wprep")
    NCH = 1 if dbg == "attn1" else 8
    dbg_outs = {}
    if dbg is not None:
        for name, shape in (("d_r1", [KB, 128, TC]), ("d_y1", [KB, 128, TC]),
                            ("d_h", [TAU, 128, TC]), ("d_r2", [KB, 128, TC]),
                            ("d_zsgu", [32, 2 * INTER]), ("d_zsdn", [TAU, D]),
                            ("d_b01t", [128, D]), ("d_cs", [32, TC]),
                            ("d_hs", [TAU, TC])):
            dbg_outs[name] = nc.dram_tensor(name, shape, BF16,
                                            kind="ExternalOutput").ap()

    es = ExitStack()

    def mksem(name):
        return SemT(es.enter_context(nc.semaphore(name)))

    qXc = mksem("qXc"); qXs = mksem("qXs"); qXt = mksem("qXt")
    qMk = mksem("qMk"); qWc = mksem("qWc"); qWs = mksem("qWs")
    qBs = mksem("qBs"); qMt = mksem("qMt"); qWt = mksem("qWt")
    qRs = mksem("qRs"); qRl = mksem("qRl"); qY1 = mksem("qY1")
    qZL = mksem("qZL"); qZS = mksem("qZS"); qYL = mksem("qYL"); qZC = mksem("qZC")
    qQG = mksem("qQG"); qSB = mksem("qSB"); qHs = mksem("qHs")
    qHl = mksem("qHl"); qH2 = mksem("qH2"); qQD = mksem("qQD")
    qSD = mksem("qSD"); qYl = mksem("qYl"); qR2s = mksem("qR2s")
    qR2l = mksem("qR2l"); qOut = mksem("qOut"); qMisc = mksem("qMisc")
    qHlA = mksem("qHlA"); qHlB = mksem("qHlB")
    qHa = mksem("qHa"); qHb = mksem("qHb")
    qYa = mksem("qYa"); qYb = mksem("qYb")
    qRa = mksem("qRa"); qRb = mksem("qRb")
    qRlA = mksem("qRlA"); qRlB = mksem("qRlB")
    qXcA = mksem("qXcA"); qXcB = mksem("qXcB")
    qXsA = mksem("qXsA"); qXsB = mksem("qXsB")
    qWcA = mksem("qWcA"); qWcB = mksem("qWcB")
    qMtA = mksem("qMtA"); qMtB = mksem("qMtB")
    qRsA = mksem("qRsA"); qRsB = mksem("qRsB")
    qZL2 = mksem("qZL2")
    qQGA = mksem("qQGA"); qQGB = mksem("qQGB")
    qSBA = mksem("qSBA"); qSBB = mksem("qSBB")
    qZCA = mksem("qZCA"); qZCB = mksem("qZCB")
    qHsA = mksem("qHsA"); qHsB = mksem("qHsB")
    qSDA = mksem("qSDA"); qSDB = mksem("qSDB")
    qR2sS = [mksem(f"qR2s{j}") for j in range(4)]
    qOutA = mksem("qOutA"); qOutB = mksem("qOutB")
    vMr = mksem("vMr"); vB1 = mksem("vB1"); tB = mksem("tB")
    vWm = mksem("vWm"); tA = mksem("tA"); vAE = mksem("vAE")
    aSq = mksem("aSq"); tS1 = mksem("tS1"); vF1 = mksem("vF1")
    aF1 = mksem("aF1"); tBc = mksem("tBc"); vAp = mksem("vAp")
    tCs = mksem("tCs"); vCs = mksem("vCs"); vZB = mksem("vZB")
    vDQ = mksem("vDQ"); tMM = mksem("tMM"); aSI = mksem("aSI"); vEV = mksem("vEV")
    tHs = mksem("tHs"); vHs = mksem("vHs"); vZD = mksem("vZD")
    vDQ2 = mksem("vDQ2"); tDN = mksem("tDN"); vDE = mksem("vDE")
    tTf = mksem("tTf"); aTE = mksem("aTE")
    aSq2 = mksem("aSq2"); tS2 = mksem("tS2"); vF2 = mksem("vF2")
    aF2 = mksem("aF2"); tBc2 = mksem("tBc2"); vAp2 = mksem("vAp2")
    vSS = mksem("vSS"); qFl = mksem("qFl")

    with es:
        toep = es.enter_context(nc.sbuf_tensor("toep_s", [128, 173], BF16))
        onesr = es.enter_context(nc.sbuf_tensor("onesr_s", [1, 128], BF16))
        hsum = es.enter_context(nc.sbuf_tensor("hsum", [TAU, TC], BF16))
        row_m = es.enter_context(nc.sbuf_tensor("row_m", [1, TC], F32))
        row_a = es.enter_context(nc.sbuf_tensor("row_a", [1, TC], F32))
        row_b = es.enter_context(nc.sbuf_tensor("row_b", [1, TC], F32))
        row_c = es.enter_context(nc.sbuf_tensor("row_c", [1, TC], F32))
        row_rsb = es.enter_context(nc.sbuf_tensor("row_rsb", [1, TC], BF16))
        row_mrb = es.enter_context(nc.sbuf_tensor("row_mrb", [1, TC], BF16))
        fl_s = es.enter_context(nc.sbuf_tensor("fl_s", [1, 32], BF16))
        fl_v = es.enter_context(nc.sbuf_tensor("fl_v", [1, 32], BF16))
        fl_c = es.enter_context(nc.sbuf_tensor("fl_c", [1, 32], BF16))

        def vfl(ve, sem, k=1):
            ve.tensor_copy(fl_v[:], fl_s[:]).then_inc(sem.h, k)

        def afl(ac, sem, k=1):
            ac.activation(fl_c[:], fl_s[:], AF.Copy).then_inc(sem.h, k)
        csum = es.enter_context(nc.sbuf_tensor("csum", [32, TC], BF16))
        zsd = es.enter_context(nc.sbuf_tensor("zsd", [TAU, D], BF16))

        with nc.Block() as block:
            def c_gp(gp):
                gp.dma_start(out=toep[:], in_=toep_in).then_inc(qMisc.h, 16)
                gp.dma_start(out=onesr[:], in_=ones_in).then_inc(qMisc.h, 16)
                gp.dma_start(out=fl_s[:], in_=ones_in[:, 0:32]
                             ).then_inc(qFl.h, 16)
            block.gpsimd(c_gp)
            qMisc.n += 32

            esA = ExitStack()
            bsel = esA.enter_context(nc.sbuf_tensor("bsel_s", [128, 256], BF16))
            eye = esA.enter_context(nc.sbuf_tensor("eye_s", [128, 128], BF16))
            xT = esA.enter_context(nc.sbuf_tensor("xT", [128, KB * TC], BF16,
                                                  is_transpose_dest=True))
            xc = [esA.enter_context(nc.sbuf_tensor(f"xc{i}", [128, D], BF16))
                  for i in range(2)]
            wc = [esA.enter_context(nc.sbuf_tensor(f"wc{i}", [128, D], BF16))
                  for i in range(2)]
            mk = [esA.enter_context(nc.sbuf_tensor(f"mk{i}", [128, D], I32))
                  for i in range(1)]
            s1i = esA.enter_context(nc.sbuf_tensor("s1i", [128, 128], F32))
            s1b = esA.enter_context(nc.sbuf_tensor("s1b", [128, 128], BF16))
            b01 = esA.enter_context(nc.sbuf_tensor("b01", [128, 128], BF16))
            wcm = [esA.enter_context(nc.sbuf_tensor(f"wcm{i}", [128, D], BF16))
                   for i in range(2)]
            mkb = [esA.enter_context(nc.sbuf_tensor(f"mkb{i}", [128, 128], BF16))
                   for i in range(2)]
            wT = [esA.enter_context(nc.sbuf_tensor(f"wT{i}", [128, KB * 256], BF16))
                  for i in range(2)]
            rst = [esA.enter_context(nc.sbuf_tensor(f"rst{i}", [128, TC], BF16))
                   for i in range(2)]
            psA = [esA.enter_context(nc.psum_tensor(f"psA{i}", [128, TC], F32))
                   for i in range(2)]
            psT = [esA.enter_context(nc.psum_tensor(f"psT{i}", [128, 512], F32))
                   for i in range(2)]
            NCH2 = 2 * NCH
            NFILL = 8
            N_WT = (2 * NCH2) if do_attn else (32 if do_w else 0)

            def pa_gp(gp):
                gp.dma_start(out=bsel[:], in_=bsel_in).then_inc(qMisc.h, 16)
                gp.dma_start(out=eye[:], in_=eye_in).then_inc(qMisc.h, 16)
                for i in range(8 if do_x else 0):
                    if i >= 2:
                        gp.wait_ge((qXsA if i % 2 == 0 else qXsB).h,
                                   16 * ((i - 2) // 2 + 1))
                    s = qXcA if i % 2 == 0 else qXcB
                    gp.dma_start(out=xc[i % 2][:],
                                 in_=x_in[128 * i:128 * (i + 1), :]
                                 ).then_inc(s.h, 16)
                    s.n += 16
                if do_x:
                    gp.wait_ge(qXt.h, KB * 16)
                for i in range(N_WT):
                    if i >= 2:
                        gp.wait_ge(vWm.h, i - 1)
                    s = qWcA if i % 2 == 0 else qWcB
                    gp.dma_start(out=wc[i % 2][:],
                                 in_=w_in[128 * i:128 * (i + 1), :]
                                 ).then_inc(s.h, 16)
                    s.n += 16
                    if i == 0:
                        gp.wait_ge(qBs.h, 16)
                    s = qMtA if i % 2 == 0 else qMtB
                    gp.dma_start(out=mkb[i % 2][:],
                                 in_=b01t_d[4 * i:4 * i + 4, 0:128]
                                 .unsqueeze(1).broadcast_to((4, 32, 128))
                                 ).then_inc(s.h, 16)
                    s.n += 16
            block.gpsimd(pa_gp)
            qMisc.n += 32

            def pa_sc(sc):
                for i in range(8 if do_x else 0):
                    sc.wait_ge((qXcA if i % 2 == 0 else qXcB).h,
                               16 * (i // 2 + 1))
                    s = qXsA if i % 2 == 0 else qXsB
                    sc.dma_start(out=xbf_d[128 * i:128 * (i + 1), :],
                                 in_=xc[i % 2][:]).then_inc(s.h, 16)
                    s.n += 16
                if do_x:
                    sc.wait_ge(qXt.h, KB * 16)
                sc.wait_ge(vB1.h, 4)
                sc.dma_start(out=b01t_d[0:128, 0:128], in_=b01[:]
                             ).then_inc(qBs.h, 16)
            block.scalar(pa_sc)
            qBs.n += 16

            def pa_sy(sy):
                if do_x:
                    sy.wait_ge(qXsA.h, qXsA.n)
                    sy.wait_ge(qXsB.h, qXsB.n)
                for b in range(KB if do_x else 0):
                    sy.dma_start_transpose(out=xT[:, TC * b:TC * (b + 1)],
                                           in_=xbf_d[:, 128 * b:128 * (b + 1)]
                                           ).then_inc(qXt.h, 16)
                for i in range(32):
                    if i >= 1:
                        sy.wait_ge(vMr.h, i)
                    sy.dma_start(out=mk[0][:],
                                 in_=m_in[128 * i:128 * (i + 1), :]
                                 ).then_inc(qMk.h, 16)
                for i in range(2 * NCH2 if do_attn else 0):
                    sy.wait_ge(vAE.h, i + 1)
                    s = qRsA if i % 2 == 0 else qRsB
                    sy.dma_start(out=r1_d[i], in_=rst[i % 2][:]
                                 ).then_inc(s.h, 16)
                    s.n += 16
            block.sync(pa_sy)
            qXt.n += KB * 16 if do_x else 0
            qMk.n += 32 * 16

            def pa_ve(ve):
                ve.wait_ge(qFl.h, 16)
                for i in range(32):
                    ve.wait_ge(qMk.h, 16 * (i + 1))
                    src = mk[0][:].rearrange("p (k r) -> p k r", r=32)
                    ve.tensor_reduce(s1i[:], src, AX.X, ALU.add
                                     ).then_inc(vSS.h, 1)
                    vSS.n += 1
                    ve.wait_ge(vSS.h, vSS.n)
                    if i >= 1:
                        ve.wait_ge(tB.h, i)
                    ins_ = ve.tensor_copy(s1b[:], s1i[:])
                    ins_.then_inc(vMr.h, 1)
                    if i % 8 == 7:
                        q = i // 8
                        ve.wait_ge(tB.h, 8 * (q + 1))
                        ins_ = ve.tensor_single_scalar(b01[32 * q:32 * (q + 1), :],
                                                psA[1][0:32, 896:1024],
                                                0.0, ALU.is_gt)
                        ins_.then_inc(vB1.h, 1)
                def emit_tt(w):
                    ve.wait_ge((qWcA if w % 2 == 0 else qWcB).h,
                               16 * (w // 2 + 1))
                    ve.wait_ge((qMtA if w % 2 == 0 else qMtB).h,
                               16 * (w // 2 + 1))
                    if w >= 2:
                        ve.wait_ge(tTf.h, NFILL * (w - 1))
                    ins_ = ve.tensor_mul(
                        wcm[w % 2][:].rearrange("p (k r) -> p k r", r=32),
                        wc[w % 2][:].rearrange("p (k r) -> p k r", r=32),
                        mkb[w % 2][:].unsqueeze(2).broadcast_to((128, 128, 32)))
                    ins_.then_inc(vWm.h, 1)

                if do_attn:
                    for w0 in range(min(6, N_WT)):
                        emit_tt(w0)
                    for j in range(NCH2):
                        ve.wait_ge(tA.h, 32 * (j + 1))
                        for nt in range(2):
                            i = j * 2 + nt
                            if i >= 2:
                                ve.wait_ge((qRsA if i % 2 == 0 else qRsB).h,
                                           16 * ((i - 2) // 2 + 1))
                            ins_ = ve.tensor_add(rst[i % 2][:], psA[nt][:],
                                          xT[:, TC * i:TC * (i + 1)])
                            ins_.then_inc(vAE.h, 1)
                        for w2 in (2 * (j + 3), 2 * (j + 3) + 1):
                            if 6 <= w2 < N_WT:
                                emit_tt(w2)
                else:
                    for w in range(N_WT):
                        emit_tt(w)
            block.vector(pa_ve)
            vMr.n += 32
            vB1.n += 4
            vWm.n += N_WT
            vAE.n += 2 * NCH2 if do_attn else 0

            def pa_ac(ac):
                ac.wait_ge(qFl.h, 16)
                for f in range(NFILL * N_WT):
                    i = f // NFILL
                    fl2_ = f % NFILL
                    j = i // 2
                    t2 = i % 2
                    g4 = fl2_
                    ac.wait_ge(tTf.h, f + 1)
                    if do_attn and j >= 2:
                        ac.wait_ge(tA.h, 32 * (j - 1))
                    dst = wT[j % 2][:, 4096 * t2 + 512 * g4:
                                     4096 * t2 + 512 * (g4 + 1)]
                    ins_ = ac.activation(dst, psT[f % 2][:], AF.Copy)
                    ins_.then_inc(aTE.h, 1)
            block.scalar(pa_ac)
            aTE.n += NFILL * N_WT

            def pa_te(te):
                te.wait_ge(qMisc.h, 64)
                for i in range(32):
                    q, r = divmod(i, 8)
                    te.wait_ge(vMr.h, i + 1)
                    if r == 0 and q > 0:
                        te.wait_ge(vB1.h, q)
                    te.matmul(psA[1][0:32, 896:1024], bsel[:, 32 * r:32 * (r + 1)],
                              s1b[:], start=(r == 0), stop=(r == 7)
                              ).then_inc(tB.h, 1)
                def emit_fills(i):
                    te.wait_ge(vWm.h, i + 1)
                    for g4 in range(NFILL):
                        f = i * NFILL + g4
                        if f >= 2:
                            te.wait_ge(aTE.h, f - 1)
                        ins = None
                        for cc in range(4):
                            c = 4 * g4 + cc
                            ins = te.matmul(psT[f % 2][:, 128 * cc:128 * (cc + 1)],
                                            wcm[i % 2][:, 128 * c:128 * (c + 1)],
                                            eye[:], start=(cc == 0),
                                            stop=(cc == 3), skip_group_check=True)
                        ins.then_inc(tTf.h, 1)
                    te.then_last = None

                for i0 in range(min(4, N_WT)):
                    emit_fills(i0)
                if do_attn:
                    te.wait_ge(qXt.h, KB * 16)
                    te.wait_ge(vB1.h, 4)
                for j in range(NCH2 if do_attn else 0):
                    if j > 0:
                        te.wait_ge(vAE.h, 2 * j)
                    te.wait_ge(aTE.h, NFILL * 2 * (j + 1))
                    for kb in range(KB):
                        ins = None
                        for nt in range(2):
                            for hf in range(2):
                                ins = te.matmul(
                                    psA[nt][:, 512 * hf:512 * (hf + 1)],
                                    wT[j % 2][:, 4096 * nt + 128 * kb:
                                              4096 * nt + 128 * (kb + 1)],
                                    xT[:, TC * kb + 512 * hf:
                                       TC * kb + 512 * (hf + 1)],
                                    start=(kb == 0), stop=(kb == KB - 1))
                        ins.then_inc(tA.h, 1)
                    for i2 in (2 * (j + 2), 2 * (j + 2) + 1):
                        if 4 <= i2 < N_WT:
                            emit_fills(i2)
                if not do_attn and do_w:
                    for i in range(4, N_WT):
                        emit_fills(i)
            block.tensor(pa_te)
            tB.n += 32
            tTf.n += NFILL * N_WT
            tA.n += 32 * NCH2 if do_attn else 0

            if dbg in ("mask", "xonly", "wprep", "attn", "attn1"):
                def dbg_gp(gp):
                    n = 0
                    if do_attn:
                        gp.wait_ge(qRsA.h, qRsA.n)
                        gp.wait_ge(qRsB.h, qRsB.n)
                        gp.dma_start(out=dbg_outs["d_r1"][0:4 * NCH],
                                     in_=r1_d[0:4 * NCH]
                                     ).then_inc(qMisc.h, 16)
                        n += 16
                    gp.wait_ge(qBs.h, qBs.n)
                    gp.dma_start(out=dbg_outs["d_b01t"][:], in_=b01t_d[:]
                                 ).then_inc(qMisc.h, 16)
                    n += 16
                    if do_x:
                        gp.wait_ge(qXt.h, qXt.n)
                        for b in range(KB):
                            gp.dma_start(out=dbg_outs["d_y1"][b],
                                         in_=xT[:, TC * b:TC * (b + 1)]
                                         ).then_inc(qMisc.h, 16)
                        n += 16 * KB
                    if do_w and not do_attn:
                        gp.wait_ge(aTE.h, aTE.n)
                        gp.dma_start(
                            out=dbg_outs["d_r2"][0:8].transpose([1, 0, 2]),
                            in_=wT[1][:, 0:8192].rearrange(
                                "p (b t) -> p b t", b=8)
                            ).then_inc(qMisc.h, 16)
                        n += 16
                    gp.wait_ge(qMisc.h, qMisc.n + n)
                block.gpsimd(dbg_gp)
                esA.close()
                return
            esA.close()

            esB = ExitStack()
            rsl = esB.enter_context(nc.sbuf_tensor("rsl", [128, KB * TC], BF16))
            y1T = esB.enter_context(nc.sbuf_tensor("y1T", [128, KB * TC], BF16))
            rbc = esB.enter_context(nc.sbuf_tensor("rbc", [128, TC], BF16))
            mrbc = esB.enter_context(nc.sbuf_tensor("mrbc", [128, TC], BF16))
            rsq = [esB.enter_context(nc.sbuf_tensor(f"rsq{i}", [128, TC], BF16))
                   for i in range(2)]
            tmpT = esB.enter_context(nc.sbuf_tensor("tmpT", [128, TC], BF16))
            psS = esB.enter_context(nc.psum_tensor("psS", [128, TC], F32))
            psR = esB.enter_context(nc.psum_tensor("psR", [128, TC], F32))
            psM = esB.enter_context(nc.psum_tensor("psM", [128, TC], F32))
            psC = esB.enter_context(nc.psum_tensor("psC", [128, TC], F32))

            def lb_sy(sy):
                sy.wait_ge(qRsA.h, qRsA.n)
                sy.wait_ge(qRsB.h, qRsB.n)
                sy.dma_start(
                    out=rsl[:, 0:16 * TC].rearrange("p (b t) -> p b t", b=16),
                    in_=r1_d[0:16].transpose([1, 0, 2])).then_inc(qRlA.h, 16)
                sy.dma_start(
                    out=rsl[:, 16 * TC:32 * TC].rearrange("p (b t) -> p b t",
                                                          b=16),
                    in_=r1_d[16:32].transpose([1, 0, 2])).then_inc(qRlB.h, 16)
            block.sync(lb_sy)
            qRlA.n += 16
            qRlB.n += 16

            def lb_ac(ac):
                for dt in range(KB):
                    ac.wait_ge((qRlA if dt < 16 else qRlB).h, 16)
                    if dt >= 2:
                        ac.wait_ge(tS1.h, dt - 1)
                    ins_ = ac.activation(rsq[dt % 2][:], rsl[:, TC * dt:TC * (dt + 1)],
                                  AF.Square)
                    ins_.then_inc(aSq.h, 1)
                ac.wait_ge(vF1.h, 4)
                ins_ = ac.activation(row_b[:], row_a[:], AF.Sqrt)
                ins_.then_inc(aF1.h, 1)
                ac.wait_ge(tBc.h, 4)
                ac.activation(rbc[:], psR[:], AF.Copy)
                ins_ = ac.activation(mrbc[:], psM[:], AF.Copy)
                ins_.then_inc(aF1.h, 2)
            block.scalar(lb_ac)
            aSq.n += KB
            aF1.n += 3

            def lb_te(te):
                for dt in range(KB):
                    te.wait_ge(aSq.h, dt + 1)
                    ins = None
                    for hf in range(2):
                        sl = slice(512 * hf, 512 * (hf + 1))
                        ins = te.matmul(
                            psS[0:64, sl], toep[:, 86:150],
                            rsl[:, TC * dt + 512 * hf:TC * dt + 512 * (hf + 1)],
                            start=(dt == 0), stop=False, skip_group_check=True)
                        ins = te.matmul(
                            psS[0:64, sl], toep[:, 54:118], rsq[dt % 2][:, sl],
                            start=False, stop=(dt == KB - 1), skip_group_check=True)
                    ins.then_inc(tS1.h, 1)
                te.wait_ge(vF1.h, 7)
                for hf in range(2):
                    sl = slice(512 * hf, 512 * (hf + 1))
                    te.matmul(psR[:, sl], onesr[:], row_rsb[:, sl], start=True,
                              stop=True).then_inc(tBc.h, 1)
                    te.matmul(psM[:, sl], onesr[:], row_mrb[:, sl], start=True,
                              stop=True).then_inc(tBc.h, 1)
                for dt in range(KB):
                    te.wait_ge(vAp.h, 2 * (dt + 1))
                    ins = None
                    for hf in range(2):
                        ins = te.matmul(
                            psC[0:32, 512 * hf:512 * (hf + 1)],
                            toep[:, 86 - dt:118 - dt],
                            y1T[:, TC * dt + 512 * hf:TC * dt + 512 * (hf + 1)],
                            start=(dt == 0), stop=(dt == KB - 1))
                    ins.then_inc(tCs.h, 1)
            block.tensor(lb_te)
            tS1.n += KB
            tBc.n += 4
            tCs.n += KB

            def lb_ve(ve):
                ve.wait_ge(tS1.h, KB)
                ve.tensor_scalar_mul(row_m[:], psS[0:1, :], 1.0 / D)
                ve.tensor_scalar_mul(row_a[:], psS[32:33, :], 1.0 / D)
                ve.tensor_mul(row_b[:], row_m[:], row_m[:])
                ve.tensor_sub(row_c[:], row_a[:], row_b[:])
                ins_ = ve.tensor_scalar_add(row_a[:], row_c[:], EPS)
                ins_.then_inc(vF1.h, 4)
                ve.wait_ge(aF1.h, 1)
                ve.reciprocal(row_c[:], row_b[:])
                ve.tensor_mul(row_a[:], row_m[:], row_c[:])
                ve.tensor_copy(row_rsb[:], row_c[:])
                ins_ = ve.tensor_copy(row_mrb[:], row_a[:])
                ins_.then_inc(vF1.h, 3)
                ve.wait_ge(aF1.h, 3)
                for dt in range(KB):
                    sl = slice(TC * dt, TC * (dt + 1))
                    ve.tensor_mul(tmpT[:], rsl[:, sl], rbc[:])
                    ins_ = ve.tensor_sub(y1T[:, sl], tmpT[:], mrbc[:])
                    ins_.then_inc(vAp.h, 2)
                ve.wait_ge(tCs.h, KB)
                ins_ = ve.tensor_copy(csum[:], psC[0:32, :])
                ins_.then_inc(vCs.h, 1)
            block.vector(lb_ve)
            vF1.n += 7
            vAp.n += 2 * KB
            vCs.n += 1

            def lb_sc(sc):
                for dt in range(KB):
                    sc.wait_ge(vAp.h, 2 * (dt + 1))
                    sc.dma_start(out=y1_d[dt], in_=y1T[:, TC * dt:TC * (dt + 1)]
                                 ).then_inc(qY1.h, 16)
            block.scalar(lb_sc)
            qY1.n += KB * 16

            if dbg == "ln1":
                def dbg_gp(gp):
                    gp.wait_ge(qY1.h, qY1.n)
                    gp.wait_ge(vCs.h, vCs.n)
                    gp.dma_start(out=dbg_outs["d_y1"][:], in_=y1_d[:]
                                 ).then_inc(qMisc.h, 16)
                    gp.dma_start(out=dbg_outs["d_cs"][:], in_=csum[:]
                                 ).then_inc(qMisc.h, 16)
                    gp.wait_ge(qMisc.h, qMisc.n + 32)
                block.gpsimd(dbg_gp)
                esB.close()
                return
            esB.close()

            esY = ExitStack()
            y1T2 = esY.enter_context(nc.sbuf_tensor("y1T2", [128, KB * TC], BF16))
            esZ = ExitStack()
            qzt = esZ.enter_context(nc.sbuf_tensor("qzt", [32, 688], I32))
            znb = esZ.enter_context(nc.sbuf_tensor("znb", [32, 5504], I32))
            sst = esZ.enter_context(nc.sbuf_tensor("sst", [32, 5504], BF16))
            zst = esZ.enter_context(nc.sbuf_tensor("zst", [32, 5504], BF16))
            qzt2 = esZ.enter_context(nc.sbuf_tensor("qzt2", [TAU, 512], I32))
            znb2 = esZ.enter_context(nc.sbuf_tensor("znb2", [TAU, D], I32))
            sst2 = esZ.enter_context(nc.sbuf_tensor("sst2", [TAU, D], BF16))

            def z_gp(gp):
                gp.wait_ge(qY1.h, qY1.n)
                gp.wait_ge(vCs.h, vCs.n)
                for ch in range(4):
                    if ch >= 1:
                        gp.wait_ge(vZB.h, 9 * ch)
                    gp.dma_start(out=qzt[:],
                                 in_=qzgu_in[:, 688 * ch:688 * (ch + 1)]
                                 ).then_inc(qZL.h, 16)
                    gp.dma_start(out=sst[:],
                                 in_=sgu_in[:, 5504 * ch:5504 * (ch + 1)]
                                 ).then_inc(qZL.h, 16)
                gp.dma_start(out=qzt2[:], in_=qzdn_in).then_inc(qZL2.h, 16)
                gp.dma_start(out=sst2[:], in_=sdn_in).then_inc(qZL2.h, 16)
                for dt in range(KB):
                    gp.dma_start(out=y1T2[:, TC * dt:TC * (dt + 1)], in_=y1_d[dt]
                                 ).then_inc(qYL.h, 16)
            block.gpsimd(z_gp)
            qZL.n += 8 * 16
            qZL2.n += 2 * 16
            qYL.n += KB * 16

            def z_ve(ve):
                for ch in range(4):
                    ve.wait_ge(qZL.h, 32 * (ch + 1))
                    if ch >= 1:
                        ve.wait_ge(qZS.h, 16 * ch)
                    for j in range(8):
                        ve.tensor_scalar(znb[:, 688 * j:688 * (j + 1)], qzt[:],
                                         4 * j, 15, ALU.logical_shift_right,
                                         ALU.bitwise_and).then_inc(vZB.h, 1)
                    nv = znb[:].rearrange("p (j q) -> p q j", j=8)
                    ins_ = ve.scalar_tensor_tensor(zst[:], nv, -1.0, sst[:], ALU.mult,
                                            ALU.mult)
                    ins_.then_inc(vZB.h, 1)
                ve.wait_ge(qZL.h, 128)
                ve.wait_ge(qZL2.h, 32)
                for j in range(8):
                    ve.tensor_scalar(znb2[:, 512 * j:512 * (j + 1)], qzt2[:],
                                     4 * j, 15, ALU.logical_shift_right,
                                     ALU.bitwise_and)
                nv2 = znb2[:].rearrange("p (j q) -> p q j", j=8)
                ins_ = ve.scalar_tensor_tensor(zsd[:], nv2, -1.0, sst2[:], ALU.mult,
                                        ALU.mult)
                ins_.then_inc(vZD.h, 1)
            block.vector(z_ve)
            vZB.n += 36
            vZD.n += 1

            def z_sc(sc):
                for ch in range(4):
                    sc.wait_ge(vZB.h, 9 * (ch + 1))
                    sc.dma_start(out=zsgu_d[:, 5504 * ch:5504 * (ch + 1)],
                                 in_=zst[:]).then_inc(qZS.h, 16)
            block.scalar(z_sc)
            qZS.n += 4 * 16
            esZ.close()

            esC = ExitStack()
            qg = [esC.enter_context(nc.sbuf_tensor(f"qg{i}", [128, 1024], I32))
                  for i in range(2)]
            nib = esC.enter_context(nc.sbuf_tensor("nib", [128, 2048], I32))
            deq = [esC.enter_context(nc.sbuf_tensor(f"deq{i}", [128, 8192], BF16))
                   for i in range(2)]
            sbt = [esC.enter_context(nc.sbuf_tensor(f"sbt{i}", [128, 8192], BF16))
                   for i in range(3)]
            zsg = [esC.enter_context(nc.sbuf_tensor(f"zsg{i}", [32, 1024], BF16))
                   for i in range(2)]
            hst = [esC.enter_context(nc.sbuf_tensor(f"hst{i}", [128, TC], BF16))
                   for i in range(2)]
            hso = [esC.enter_context(nc.sbuf_tensor(f"hso{i}", [128, TC], BF16))
                   for i in range(2)]
            psG = [esC.enter_context(nc.psum_tensor(f"psG{i}", [128, TC], F32))
                   for i in range(2)]
            psU = [esC.enter_context(nc.psum_tensor(f"psU{i}", [128, TC], F32))
                   for i in range(2)]

            def c_gp2(gp):
                gp.wait_ge(qZS.h, qZS.n)
                gp.wait_ge(qYL.h, qYL.n)
                gp.wait_ge(vZD.h, vZD.n)
                NZC4 = (TAU + 3) // 4
                for c in range(2):
                    t0 = 4 * c
                    tn = min(4, TAU - t0)
                    s = qZCA if c % 2 == 0 else qZCB
                    gp.dma_start(out=zsg[c % 2][:, 0:128 * tn],
                                 in_=zsgu_d[:, 128 * t0:128 * (t0 + tn)]
                                 ).then_inc(s.h, 16)
                    gp.dma_start(out=zsg[c % 2][:, 512:512 + 128 * tn],
                                 in_=zsgu_d[:, INTER + 128 * t0:
                                            INTER + 128 * (t0 + tn)]
                                 ).then_inc(s.h, 16)
                    s.n += 32
                for tau in range(TAU):
                    if tau % 4 == 0 and tau >= 4:
                        c = tau // 4 + 1
                        if c < NZC4:
                            t0 = 4 * c
                            tn = min(4, TAU - t0)
                            gp.wait_ge(tMM.h, 4 * (c - 1))
                            s = qZCA if c % 2 == 0 else qZCB
                            gp.dma_start(out=zsg[c % 2][:, 0:128 * tn],
                                         in_=zsgu_d[:, 128 * t0:128 * (t0 + tn)]
                                         ).then_inc(s.h, 16)
                            gp.dma_start(out=zsg[c % 2][:, 512:512 + 128 * tn],
                                         in_=zsgu_d[:, INTER + 128 * t0:
                                                    INTER + 128 * (t0 + tn)]
                                         ).then_inc(s.h, 16)
                            s.n += 32
                    if tau >= 3:
                        gp.wait_ge(vDQ.h, 2 * (tau - 2))
                    s = qSBA if tau % 2 == 0 else qSBB
                    for g in range(2):
                        src = sgu_in[:, INTER * g + 128 * tau:
                                     INTER * g + 128 * (tau + 1)]
                        src = src.unsqueeze(0).broadcast_to((128, 32, 128))
                        gp.dma_start(out=sbt[tau % 3][:, 4096 * g:4096 * (g + 1)],
                                     in_=src).then_inc(s.h, 16)
                        s.n += 16
                    if tau >= 2:
                        gp.wait_ge(vDQ.h, 2 * (tau - 1))
                    s = qQGA if tau % 2 == 0 else qQGB
                    gp.dma_start(out=qg[tau % 2][:],
                                 in_=qgu_in[tau].transpose([1, 0, 2])
                                 ).then_inc(s.h, 16)
                    s.n += 16
            block.gpsimd(c_gp2)

            def c_ve(ve):
                for tau in range(TAU):
                    par = tau % 2
                    if tau >= 2:
                        t2 = tau - 2
                        ve.wait_ge(aSI.h, t2 + 1)
                        if t2 >= 2:
                            ve.wait_ge((qHsA if t2 % 2 == 0 else qHsB).h,
                                       16 * ((t2 - 2) // 2 + 1))
                        ins_ = ve.tensor_mul(hso[t2 % 2][:], hst[t2 % 2][:],
                                      psU[t2 % 2][:])
                        ins_.then_inc(vEV.h, 1)
                    ve.wait_ge((qQGA if tau % 2 == 0 else qQGB).h,
                               16 * (tau // 2 + 1))
                    ve.wait_ge((qSBA if tau % 2 == 0 else qSBB).h,
                               32 * (tau // 2 + 1))
                    if tau >= 2:
                        ve.wait_ge(tMM.h, tau - 1)
                    for g in range(2):
                        for sub in range(2):
                            off = 512 * g + 256 * sub
                            for j in range(8):
                                ve.tensor_scalar(nib[:, 256 * j:256 * (j + 1)],
                                                 qg[par][:, off:off + 256],
                                                 4 * j, 15,
                                                 ALU.logical_shift_right,
                                                 ALU.bitwise_and)
                            nv = nib[:].rearrange("p (b kb a) -> p kb a b",
                                                  b=8, kb=16)
                            ins = ve.tensor_mul(
                                deq[par][:, 4096 * g + 2048 * sub:
                                         4096 * g + 2048 * (sub + 1)],
                                nv,
                                sbt[tau % 3][:, 4096 * g + 2048 * sub:
                                         4096 * g + 2048 * (sub + 1)])
                        ins.then_inc(vDQ.h, 1)
                for t2 in (TAU - 2, TAU - 1):
                    ve.wait_ge(aSI.h, t2 + 1)
                    ve.wait_ge((qHsA if t2 % 2 == 0 else qHsB).h,
                               16 * ((t2 - 2) // 2 + 1))
                    ins_ = ve.tensor_mul(hso[t2 % 2][:], hst[t2 % 2][:],
                                  psU[t2 % 2][:])
                    ins_.then_inc(vEV.h, 1)
            block.vector(c_ve)
            vDQ.n += 2 * TAU
            vEV.n += TAU

            def c_ac(ac):
                for tau in range(TAU):
                    ac.wait_ge(tMM.h, tau + 1)
                    if tau >= 2:
                        ac.wait_ge(vEV.h, tau - 1)
                    ins_ = ac.activation(hst[tau % 2][:], psG[tau % 2][:], AF.Silu)
                    ins_.then_inc(aSI.h, 1)
                    if tau >= 2:
                        t2 = tau - 2
                        ac.wait_ge(vEV.h, t2 + 1)
                        s = qHsA if t2 % 2 == 0 else qHsB
                        ac.dma_start(out=h_d[t2], in_=hso[t2 % 2][:]
                                     ).then_inc(s.h, 16)
                        s.n += 16
                for t2 in (TAU - 2, TAU - 1):
                    ac.wait_ge(vEV.h, t2 + 1)
                    s = qHsA if t2 % 2 == 0 else qHsB
                    ac.dma_start(out=h_d[t2], in_=hso[t2 % 2][:]
                                 ).then_inc(s.h, 16)
                    s.n += 16
            block.scalar(c_ac)
            aSI.n += TAU

            def c_te(te):
                te.wait_ge(vCs.h, vCs.n)
                for tau in range(TAU):
                    par = tau % 2
                    zch = tau // 4
                    zoff = 128 * (tau - 4 * zch)
                    te.wait_ge(vDQ.h, 2 * (tau + 1))
                    te.wait_ge((qZCA if zch % 2 == 0 else qZCB).h,
                               32 * (zch // 2 + 1))
                    if tau >= 2:
                        te.wait_ge(aSI.h, tau - 1)
                        te.wait_ge(vEV.h, tau - 1)
                    for g in range(2):
                        ps = psG[par] if g == 0 else psU[par]
                        ins = None
                        for hf in range(2):
                            sl = slice(512 * hf, 512 * (hf + 1))
                            for kb in range(KB):
                                ins = te.matmul(
                                    ps[:, sl],
                                    deq[par][:, 4096 * g + 128 * kb:
                                             4096 * g + 128 * (kb + 1)],
                                    y1T2[:, TC * kb + 512 * hf:
                                         TC * kb + 512 * (hf + 1)],
                                    start=(kb == 0), stop=False)
                            ins = te.matmul(
                                ps[:, sl],
                                zsg[zch % 2][:, 512 * g + zoff:
                                             512 * g + zoff + 128],
                                csum[:, sl], start=False, stop=True,
                                skip_group_check=True)
                        if g == 1:
                            ins.then_inc(tMM.h, 1)
            block.tensor(c_te)
            tMM.n += TAU

            if dbg == "gu":
                def dbg_gp(gp):
                    gp.wait_ge(qHsA.h, qHsA.n)
                    gp.wait_ge(qHsB.h, qHsB.n)
                    gp.dma_start(out=dbg_outs["d_h"][:], in_=h_d[:]
                                 ).then_inc(qMisc.h, 16)
                    gp.dma_start(out=dbg_outs["d_zsgu"][:], in_=zsgu_d[:]
                                 ).then_inc(qMisc.h, 16)
                    gp.wait_ge(qMisc.h, qMisc.n + 32)
                block.gpsimd(dbg_gp)
                esC.close()
                esY.close()
                return
            esC.close()
            esY.close()

            if dbg == "hsum":
                NGRUN = 0
            elif dbg is not None and dbg.startswith("dng"):
                NGRUN = int(dbg[3:])
            else:
                NGRUN = len(DN_GROUPS)
            DNG = DN_GROUPS[:NGRUN]
            esD = ExitStack()
            hsl = esD.enter_context(nc.sbuf_tensor("hsl", [128, NRES * TC], BF16))
            hs2 = [esD.enter_context(nc.sbuf_tensor(f"hs2{i}", [128, 8 * TC], BF16))
                   for i in range(2)]
            qds = esD.enter_context(nc.sbuf_tensor("qds", [128, TAU * 48], I32))
            nib2 = esD.enter_context(nc.sbuf_tensor("nib2", [128, 8 * 8 * 48], I32))
            dq2 = [esD.enter_context(nc.sbuf_tensor(f"dq2{i}", [128, 8 * 384], BF16))
                   for i in range(2)]
            sdt = [esD.enter_context(nc.sbuf_tensor(f"sdt{i}", [128, 8 * 384], BF16))
                   for i in range(2)]
            y1s = [esD.enter_context(nc.sbuf_tensor(f"y1s{i}", [128, TC], BF16))
                   for i in range(2)]
            r2s = [esD.enter_context(nc.sbuf_tensor(f"r2s{i}", [128, TC], BF16))
                   for i in range(4)]
            rsq2 = [esD.enter_context(nc.sbuf_tensor(f"rsq2{i}", [128, TC], BF16))
                    for i in range(2)]
            psD3 = [esD.enter_context(nc.psum_tensor(f"psD{i}", [128, TC], F32))
                    for i in range(3)]
            psS2 = esD.enter_context(nc.psum_tensor("psS2", [128, TC], F32))
            NSCH = (TAU - NRES + 7) // 8
            NG = len(DNG)
            evac_seq = [(gi, hl) for gi, nh in enumerate(DNG)
                        for hl in range(nh)]
            NEV = len(evac_seq)
            evac_cum = [0]
            for gi, nh in enumerate(DNG):
                evac_cum.append(evac_cum[-1] + nh)

            sdn_offs = {}
            _off = 0
            for _gi, _nh in enumerate(DN_GROUPS):
                _ewg = 8 * 16 * _nh
                for _c in range(NZCH):
                    _tn = min(8, TAU - 8 * _c)
                    sdn_offs[(_gi, _c)] = _off
                    _off += _tn * _ewg

            def d_gp(gp):
                gp.wait_ge(qHsA.h, qHsA.n)
                gp.wait_ge(qHsB.h, qHsB.n)
                for gi, nh in enumerate(DNG):
                    wg = 16 * nh
                    w0 = 48 * gi
                    for c in range(NZCH):
                        t0 = 8 * c
                        tn = min(8, TAU - t0)
                        idx = gi * NZCH + c
                        if idx >= 2:
                            gp.wait_ge(vDQ2.h, idx - 1)
                        off = sdn_offs[(gi, c)]
                        s = qSDA if idx % 2 == 0 else qSDB
                        src = sdnr_in[0, off:off + tn * 8 * wg]
                        src = src.unsqueeze(0).broadcast_to((128, tn * 8 * wg))
                        gp.dma_start(out=sdt[idx % 2][:, 0:tn * 8 * wg],
                                     in_=src).then_inc(s.h, 16)
                        s.n += 16
            block.gpsimd(d_gp)

            hsum_tgt = {}
            chunk_tgt = {}
            y1s_tgt = {}

            def d_sy(sy):
                sy.wait_ge(qHsA.h, qHsA.n)
                sy.wait_ge(qHsB.h, qHsB.n)
                sy.dma_start(
                    out=hsl[:, 0:18 * TC].rearrange("p (b t) -> p b t", b=18),
                    in_=h_d[0:18].transpose([1, 0, 2])).then_inc(qHlA.h, 16)
                sy.dma_start(
                    out=hsl[:, 18 * TC:36 * TC].rearrange("p (b t) -> p b t",
                                                          b=18),
                    in_=h_d[18:36].transpose([1, 0, 2])).then_inc(qHlB.h, 16)
                for tau in range(NRES, TAU):
                    if tau >= NRES + 2:
                        sy.wait_ge(tHs.h, tau - 1)
                    s = qHa if tau % 2 == 0 else qHb
                    sy.dma_start(out=hs2[tau % 2][:, 0:TC], in_=h_d[tau]
                                 ).then_inc(s.h, 16)
                    s.n += 16
                    hsum_tgt[tau] = (s, s.n)
                for gi, nh in enumerate(DNG):
                    wg = 16 * nh
                    w0g = TAU * 48 * gi
                    if gi >= 1:
                        sy.wait_ge(vDQ2.h, NZCH * gi)
                    sy.dma_start(out=qds[:, 0:TAU * wg],
                                 in_=qdn_in[:, w0g:w0g + TAU * wg]
                                 ).then_inc(qQD.h, 16)
                    for cp in range(NSCH):
                        t0 = NRES + 8 * cp
                        tn = min(8, TAU - t0)
                        if gi == 0:
                            sy.wait_ge(tHs.h, TAU)
                        if cp >= 2:
                            cons = 6 + (cp - 2) if cp - 2 <= 4 else NZCH
                            sy.wait_ge(tDN.h, NZCH * gi + cons)
                        elif gi >= 1:
                            sy.wait_ge(tDN.h, NZCH * gi)
                        s = qHa if cp % 2 == 0 else qHb
                        sy.dma_start(out=hs2[cp % 2][:, 0:tn * TC],
                                     in_=h_d[t0:t0 + tn].transpose([1, 0, 2])
                                     ).then_inc(s.h, 16)
                        s.n += 16
                        chunk_tgt[(gi, cp)] = (s, s.n)
                    for hl in range(nh):
                        k = evac_cum[gi] + hl
                        if k >= 2:
                            sy.wait_ge(vDE.h, k - 1)
                        ht = 3 * gi + hl
                        s = qYa if k % 2 == 0 else qYb
                        sy.dma_start(out=y1s[k % 2][:], in_=y1_d[ht]
                                     ).then_inc(s.h, 16)
                        s.n += 16
                        y1s_tgt[k] = (s, s.n)
            block.sync(d_sy)
            qHlA.n += 16
            qHlB.n += 16
            qQD.n += NG * 16

            def d_ve(ve):
                ve.wait_ge(tHs.h, TAU)
                ins_ = ve.tensor_copy(hsum[:], psS2[0:TAU, :])
                ins_.then_inc(vHs.h, 1)
                for gi, nh in enumerate(DNG):
                    wg = 16 * nh
                    ve.wait_ge(qQD.h, 16 * (gi + 1))
                    for c in range(NZCH):
                        t0 = 8 * c
                        tn = min(8, TAU - t0)
                        idx = gi * NZCH + c
                        ve.wait_ge((qSDA if idx % 2 == 0 else qSDB).h,
                                   16 * (idx // 2 + 1))
                        if c >= 2:
                            ve.wait_ge(tDN.h, NZCH * gi + c - 1)
                        elif gi >= 1:
                            ve.wait_ge(tDN.h, NZCH * gi - c)
                        for j in range(8):
                            ov = nib2[:, 0:tn * 8 * wg].rearrange(
                                "p (t j w) -> p t j w", t=tn, j=8)[:, :, j, :]
                            iv = qds[:, wg * t0:wg * (t0 + tn)].rearrange(
                                "p (t w) -> p t w", t=tn)
                            ve.tensor_scalar(ov, iv, 4 * j, 15,
                                             ALU.logical_shift_right,
                                             ALU.bitwise_and)
                        for tl in range(tn):
                            nv2 = nib2[:, 0:tn * 8 * wg].rearrange(
                                "p (t j w) -> p t w j", t=tn, j=8)[:, tl]
                            sv = sdt[idx % 2][:, 8 * wg * tl:8 * wg * (tl + 1)]
                            ins_ = ve.tensor_mul(dq2[c % 2][:, 8 * wg * tl:
                                                           8 * wg * (tl + 1)],
                                                 nv2, sv)
                        ins_.then_inc(vDQ2.h, 1)
                    ve.wait_ge(tDN.h, NZCH * (gi + 1))
                    for hl in range(nh):
                        k = evac_cum[gi] + hl
                        ys, yc = y1s_tgt[k]
                        ve.wait_ge(ys.h, yc)
                        if k >= 4:
                            ve.wait_ge(qR2sS[k % 4].h, 16 * ((k - 4) // 4 + 1))
                        ins_ = ve.tensor_add(r2s[k % 4][:], psD3[hl][:], y1s[k % 2][:])
                        ins_.then_inc(vDE.h, 1)
            block.vector(d_ve)
            vHs.n += 1
            vDQ2.n += NG * NZCH
            vDE.n += NEV

            def d_te(te):
                te.wait_ge(vZD.h, vZD.n)
                for tau in range(TAU):
                    if tau == 0:
                        te.wait_ge(qHlA.h, 16)
                    elif tau == 18:
                        te.wait_ge(qHlB.h, 16)
                    elif tau >= NRES:
                        hs_, hc_ = hsum_tgt[tau]
                        te.wait_ge(hs_.h, hc_)
                    src = (hsl[:, TC * tau:TC * (tau + 1)] if tau < NRES
                           else hs2[tau % 2][:, 0:TC])
                    ins = None
                    for hf in range(2):
                        ins = te.matmul(psS2[0:TAU, 512 * hf:512 * (hf + 1)],
                                        toep[:, 86 - tau:172 - tau],
                                        src[:, 512 * hf:512 * (hf + 1)],
                                        start=(tau == 0), stop=(tau == TAU - 1))
                    ins.then_inc(tHs.h, 1)
                te.wait_ge(vHs.h, vHs.n)
                stats_k = 0

                def emit_stats_upto(upto):
                    nonlocal stats_k
                    while stats_k < upto:
                        k = stats_k
                        te.wait_ge(aSq2.h, k + 1)
                        ins = None
                        for hf in range(2):
                            sl = slice(512 * hf, 512 * (hf + 1))
                            te.matmul(psS2[0:64, sl], toep[:, 86:150],
                                      r2s[k % 4][:, sl], start=(k == 0),
                                      stop=False, skip_group_check=True)
                            ins = te.matmul(psS2[0:64, sl], toep[:, 54:118],
                                            rsq2[k % 2][:, sl], start=False,
                                            stop=(k == NEV - 1),
                                            skip_group_check=True)
                        ins.then_inc(tS2.h, 1)
                        stats_k += 1

                for gi, nh in enumerate(DNG):
                    wg = 16 * nh
                    if gi >= 1:
                        te.wait_ge(vDE.h, evac_cum[gi])
                    for tau in range(TAU):
                        c = tau // 8
                        if tau % 8 == 0:
                            te.wait_ge(vDQ2.h, NZCH * gi + c + 1)
                        if tau >= NRES and (tau - NRES) % 8 == 0:
                            cp = (tau - NRES) // 8
                            ps_, pc_ = chunk_tgt[(gi, cp)]
                            te.wait_ge(ps_.h, pc_)
                        tl = tau - 8 * c
                        if tau < NRES:
                            mv = hsl[:, TC * tau:TC * (tau + 1)]
                        else:
                            tl2 = (tau - NRES) % 8
                            mv = hs2[((tau - NRES) // 8) % 2][
                                :, TC * tl2:TC * (tl2 + 1)]
                        ins = None
                        for hl in range(nh):
                            for hf in range(2):
                                ins = te.matmul(
                                    psD3[hl][:, 512 * hf:512 * (hf + 1)],
                                    dq2[c % 2][:, 8 * wg * tl + 128 * hl:
                                               8 * wg * tl + 128 * (hl + 1)],
                                    mv[:, 512 * hf:512 * (hf + 1)],
                                    start=(tau == 0), stop=False)
                        if tau % 8 == 7 or tau == TAU - 1:
                            if tau == TAU - 1:
                                for hl in range(nh):
                                    ht = 3 * gi + hl
                                    for hf in range(2):
                                        ins = te.matmul(
                                            psD3[hl][:, 512 * hf:512 * (hf + 1)],
                                            zsd[:, 128 * ht:128 * (ht + 1)],
                                            hsum[:, 512 * hf:512 * (hf + 1)],
                                            start=False, stop=True,
                                            skip_group_check=True)
                            ins.then_inc(tDN.h, 1)
                    if gi >= 1:
                        emit_stats_upto(evac_cum[gi])
                emit_stats_upto(NEV)
            block.tensor(d_te)
            tHs.n += TAU
            tDN.n += NG * NZCH
            tS2.n += NEV

            def d_ac(ac):
                for k in range(NEV):
                    ac.wait_ge(vDE.h, k + 1)
                    if k >= 2:
                        ac.wait_ge(tS2.h, k - 1)
                    ins_ = ac.activation(rsq2[k % 2][:], r2s[k % 4][:], AF.Square)
                    ins_.then_inc(aSq2.h, 1)
                    if k >= 2:
                        k2 = k - 2
                        gi2, hl2 = evac_seq[k2]
                        ht = 3 * gi2 + hl2
                        ac.wait_ge(tS2.h, k2 + 1)
                        s = qR2sS[k2 % 4]
                        ac.dma_start(out=r2_d[ht], in_=r2s[k2 % 4][:]
                                     ).then_inc(s.h, 16)
                        s.n += 16
                for k2 in range(max(0, NEV - 2), NEV):
                    gi2, hl2 = evac_seq[k2]
                    ht = 3 * gi2 + hl2
                    ac.wait_ge(tS2.h, k2 + 1)
                    s = qR2sS[k2 % 4]
                    ac.dma_start(out=r2_d[ht], in_=r2s[k2 % 4][:]
                                 ).then_inc(s.h, 16)
                    s.n += 16
            block.scalar(d_ac)
            aSq2.n += NEV


            def d_fin_ve(ve):
                ve.wait_ge(tS2.h, NEV)
                ve.tensor_scalar_mul(row_m[:], psS2[0:1, :], 1.0 / D)
                ve.tensor_scalar_mul(row_a[:], psS2[32:33, :], 1.0 / D)
                ve.tensor_mul(row_b[:], row_m[:], row_m[:])
                ve.tensor_sub(row_c[:], row_a[:], row_b[:])
                ins_ = ve.tensor_scalar_add(row_a[:], row_c[:], EPS)
                ins_.then_inc(vF2.h, 4)
                ve.wait_ge(aF2.h, 1)
                ve.reciprocal(row_c[:], row_b[:])
                ve.tensor_mul(row_a[:], row_m[:], row_c[:])
                ve.tensor_copy(row_rsb[:], row_c[:])
                ins_ = ve.tensor_copy(row_mrb[:], row_a[:])
                ins_.then_inc(vF2.h, 3)
            block.vector(d_fin_ve)
            vF2.n += 7

            def d_fin_ac(ac):
                ac.wait_ge(vF2.h, 4)
                ins_ = ac.activation(row_b[:], row_a[:], AF.Sqrt)
                ins_.then_inc(aF2.h, 1)
            block.scalar(d_fin_ac)
            aF2.n += 1

            if dbg is not None:
                def dbg_gp(gp):
                    for s in qR2sS:
                        gp.wait_ge(s.h, s.n)
                    gp.wait_ge(vHs.h, vHs.n)
                    gp.wait_ge(vZD.h, vZD.n)
                    n = 48
                    gp.dma_start(out=dbg_outs["d_r2"][:], in_=r2_d[:]
                                 ).then_inc(qMisc.h, 16)
                    gp.dma_start(out=dbg_outs["d_hs"][:], in_=hsum[:]
                                 ).then_inc(qMisc.h, 16)
                    gp.dma_start(out=dbg_outs["d_zsdn"][:], in_=zsd[:]
                                 ).then_inc(qMisc.h, 16)
                    if dbg == "hsum":
                        gp.dma_start(
                            out=dbg_outs["d_y1"][:].transpose([1, 0, 2]),
                            in_=hsl[:, 0:32 * TC].rearrange(
                                "p (b t) -> p b t", b=32)
                            ).then_inc(qMisc.h, 16)
                        gp.dma_start(
                            out=dbg_outs["d_r1"][0:4].transpose([1, 0, 2]),
                            in_=hsl[:, 32 * TC:36 * TC].rearrange(
                                "p (b t) -> p b t", b=4)
                            ).then_inc(qMisc.h, 16)
                        n += 32
                    gp.wait_ge(qMisc.h, qMisc.n + n)
                block.gpsimd(dbg_gp)
                esD.close()
                return
            esD.close()

            esE = ExitStack()
            r2l = [esE.enter_context(nc.sbuf_tensor(f"r2l{i}", [128, TC], BF16))
                   for i in range(2)]
            otile = [esE.enter_context(nc.sbuf_tensor(f"ot{i}", [128, TC], F32))
                     for i in range(2)]
            tmp2 = esE.enter_context(nc.sbuf_tensor("tmp2", [128, TC], BF16))
            rbc2 = esE.enter_context(nc.sbuf_tensor("rbc2", [128, TC], BF16))
            mrbc2 = esE.enter_context(nc.sbuf_tensor("mrbc2", [128, TC], BF16))
            psR2 = esE.enter_context(nc.psum_tensor("psR2", [128, TC], F32))
            psM2 = esE.enter_context(nc.psum_tensor("psM2", [128, TC], F32))

            def e_te(te):
                te.wait_ge(vF2.h, vF2.n)
                for hf in range(2):
                    sl = slice(512 * hf, 512 * (hf + 1))
                    te.matmul(psR2[:, sl], onesr[:], row_rsb[:, sl], start=True,
                              stop=True).then_inc(tBc2.h, 1)
                    te.matmul(psM2[:, sl], onesr[:], row_mrb[:, sl], start=True,
                              stop=True).then_inc(tBc2.h, 1)
            block.tensor(e_te)
            tBc2.n += 4

            def e_ac(ac):
                ac.wait_ge(tBc2.h, 4)
                ac.activation(rbc2[:], psR2[:], AF.Copy)
                ins_ = ac.activation(mrbc2[:], psM2[:], AF.Copy)
                ins_.then_inc(aF2.h, 2)
            block.scalar(e_ac)
            aF2.n += 2

            r2l_tgt = {}

            def e_sy(sy):
                for s in qR2sS:
                    sy.wait_ge(s.h, s.n)
                for dt in range(KB):
                    if dt >= 2:
                        sy.wait_ge(vAp2.h, 2 * (dt - 1))
                    s = qRa if dt % 2 == 0 else qRb
                    sy.dma_start(out=r2l[dt % 2][:], in_=r2_d[dt]
                                 ).then_inc(s.h, 16)
                    s.n += 16
                    r2l_tgt[dt] = (s, s.n)
            block.sync(e_sy)

            def e_ve(ve):
                ve.wait_ge(aF2.h, aF2.n)
                for dt in range(KB):
                    rs_, rc_ = r2l_tgt[dt]
                    ve.wait_ge(rs_.h, rc_)
                    if dt >= 2:
                        ve.wait_ge((qOutA if dt % 2 == 0 else qOutB).h,
                                   16 * ((dt - 2) // 2 + 1))
                    ve.tensor_mul(tmp2[:], r2l[dt % 2][:], rbc2[:])
                    ins_ = ve.tensor_sub(otile[dt % 2][:], tmp2[:], mrbc2[:])
                    ins_.then_inc(vAp2.h, 2)
            block.vector(e_ve)
            vAp2.n += 2 * KB

            def e_sc(sc):
                for dt in range(KB):
                    sc.wait_ge(vAp2.h, 2 * (dt + 1))
                    s = qOutA if dt % 2 == 0 else qOutB
                    sc.dma_start(out=out_t[128 * dt:128 * (dt + 1), :],
                                 in_=otile[dt % 2][:]).then_inc(s.h, 16)
                    s.n += 16
            block.scalar(e_sc)

            def e_gp(gp):
                gp.wait_ge(qOutA.h, qOutA.n)
                gp.wait_ge(qOutB.h, qOutB.n)
            block.gpsimd(e_gp)
            esE.close()
    return nc



def _consts():
    toep = np.zeros((128, 173), dtype=ml_dtypes.bfloat16)
    toep[:, 86] = 1.0
    bsel = np.zeros((128, 8, 32), dtype=ml_dtypes.bfloat16)
    for r in range(8):
        for p in range(128):
            bsel[p, r, 4 * r + p // 32] = 1.0
    bsel = bsel.reshape(128, 256)
    onesr = np.ones((1, 128), dtype=ml_dtypes.bfloat16)
    eye = np.eye(128, dtype=np.float32).astype(ml_dtypes.bfloat16)
    return toep, bsel, onesr, eye


def _prep_inputs(L0_attn_data, L0_attn_weight, L0_attn_mask, L0_ffn_qweight_gu,
                 L0_ffn_qzeros_gu, L0_ffn_scales_gu, L0_ffn_qweight_dn,
                 L0_ffn_qzeros_dn, L0_ffn_scales_dn):
    x = np.ascontiguousarray(
        np.asarray(L0_attn_data, dtype=np.float32).reshape(8192, D))
    w = np.ascontiguousarray(np.asarray(L0_attn_weight, dtype=np.float32))
    m = np.ascontiguousarray(np.asarray(L0_attn_mask, dtype=np.int32))
    qgu = np.asarray(L0_ffn_qweight_gu, dtype=np.int32)
    qgu_r = np.ascontiguousarray(
        qgu.reshape(32, 128, 2, 86, 16).transpose(3, 2, 1, 0, 4
                                                  ).reshape(86, 2, 128, 512))
    qzgu = np.ascontiguousarray(np.asarray(L0_ffn_qzeros_gu, dtype=np.int32))
    sgu = np.ascontiguousarray(np.asarray(L0_ffn_scales_gu, dtype=np.float32))
    qdn = np.asarray(L0_ffn_qweight_dn, dtype=np.int32)
    A = qdn.reshape(86, 128, 512).transpose(1, 0, 2)
    chunks = [A[:, :, 48 * g:48 * g + 48] for g in range(10)] + [A[:, :, 480:512]]
    qdn_r = np.ascontiguousarray(
        np.concatenate([c.reshape(128, -1) for c in chunks], axis=1))
    qzdn = np.ascontiguousarray(np.asarray(L0_ffn_qzeros_dn, dtype=np.int32))
    sdn = np.ascontiguousarray(np.asarray(L0_ffn_scales_dn, dtype=np.float32))
    sgu_r = np.ascontiguousarray(
        sgu.reshape(32, 2, TAU, 128).transpose(2, 1, 0, 3).reshape(TAU, 8192))
    parts = []
    for gi, nh in enumerate(DN_GROUPS):
        ewg = 8 * 16 * nh
        col0 = 384 * gi
        for c in range(NZCH):
            t0 = 8 * c
            tn = min(8, TAU - t0)
            parts.append(sdn[t0:t0 + tn, col0:col0 + ewg].reshape(-1))
    sdn_r = np.ascontiguousarray(np.concatenate(parts))[None, :]
    toep, bselc, onesr, eye = _consts()
    shared = dict(w=w, m=m, qgu=qgu_r, qzgu=qzgu, sgu=sgu, qdn=qdn_r,
                  qzdn=qzdn, sdn=sdn, sgur=sgu_r, sdnr=sdn_r,
                  toep=np.asarray(toep),
                  bsel=np.asarray(bselc), onesr=np.asarray(onesr),
                  eye=np.asarray(eye))
    in_maps = []
    for c in range(NCORE):
        im = dict(shared)
        im["x"] = np.ascontiguousarray(x[TC * c:TC * (c + 1)])
        in_maps.append(im)
    return in_maps


def kernel(**inputs):
    in_maps = _prep_inputs(**inputs)
    nc = bass.Bass("TRN2", target_bir_lowering=False)
    build(nc)
    res = run_bass_kernel_spmd(nc, in_maps, core_ids=list(range(NCORE)))
    out = np.concatenate([r["out_t"] for r in res.results], axis=1)
    return np.ascontiguousarray(out.T).reshape(2, 4096, 4096).astype(np.float32)


# version 14
# speedup vs baseline: 1.1655x; 1.0243x over previous
import numpy as np
import ml_dtypes
import concourse.bass as bass
import concourse.mybir as mybir
from concourse.bass_utils import run_bass_kernel_spmd
from contextlib import ExitStack

F32 = mybir.dt.float32
BF16 = mybir.dt.bfloat16
I32 = mybir.dt.int32
AF = mybir.ActivationFunctionType
ALU = mybir.AluOpType
AX = mybir.AxisListType

D = 4096
KB = 32
TC = 1024
INTER = 11008
TAU = 86
NCORE = 8
EPS = 1e-5
NZCH = 11
DN_GROUPS = [3] * 10 + [2]
NRES = 36


class SemT:
    def __init__(self, h):
        self.h = h
        self.n = 0


def build(nc, dbg=None):
    xt_in = nc.dram_tensor("xt", [KB, 128, TC], F32, kind="ExternalInput").ap()
    w_in = nc.dram_tensor("w", [D, D], F32, kind="ExternalInput").ap()
    m_in = nc.dram_tensor("m", [D, D], I32, kind="ExternalInput").ap()
    qgu_in = nc.dram_tensor("qgu", [TAU, 2, 128, 512], I32, kind="ExternalInput").ap()
    qzgu_in = nc.dram_tensor("qzgu", [32, 2752], I32, kind="ExternalInput").ap()
    sgu_in = nc.dram_tensor("sgu", [32, 2 * INTER], F32, kind="ExternalInput").ap()
    qdn_in = nc.dram_tensor("qdn", [128, TAU * 512], I32, kind="ExternalInput").ap()
    qzdn_in = nc.dram_tensor("qzdn", [TAU, 512], I32, kind="ExternalInput").ap()
    sdn_in = nc.dram_tensor("sdn", [TAU, D], F32, kind="ExternalInput").ap()
    sgur_in = nc.dram_tensor("sgur", [TAU, 8192], F32, kind="ExternalInput").ap()
    sdnr_in = nc.dram_tensor("sdnr", [1, TAU * D], F32, kind="ExternalInput").ap()
    toep_in = nc.dram_tensor("toep", [128, 173], BF16, kind="ExternalInput").ap()
    bsel_in = nc.dram_tensor("bsel", [128, 256], BF16, kind="ExternalInput").ap()
    ones_in = nc.dram_tensor("onesr", [1, 128], BF16, kind="ExternalInput").ap()
    eye_in = nc.dram_tensor("eye", [128, 128], BF16, kind="ExternalInput").ap()
    out_t = nc.dram_tensor("out_t", [D, TC], F32, kind="ExternalOutput").ap()

    xbf_d = nc.dram_tensor("xbf_d", [TC, D], BF16, kind="Internal").ap()
    wbf_d = nc.dram_tensor("wbf_d", [D, D], BF16, kind="Internal").ap()
    b01t_d = nc.dram_tensor("b01t_d", [128, D], BF16, kind="Internal").ap()
    r1_d = nc.dram_tensor("r1_d", [KB, 128, TC], BF16, kind="Internal").ap()
    y1_d = nc.dram_tensor("y1_d", [KB, 128, TC], BF16, kind="Internal").ap()
    h_d = nc.dram_tensor("h_d", [TAU, 128, TC], BF16, kind="Internal").ap()
    r2_d = nc.dram_tensor("r2_d", [KB, 128, TC], BF16, kind="Internal").ap()
    zsgu_d = nc.dram_tensor("zsgu_d", [32, 2 * INTER], BF16, kind="Internal").ap()

    do_x = dbg not in ("mask",)
    do_w = dbg not in ("mask", "xonly")
    do_attn = dbg not in ("mask", "xonly", "wprep")
    NCH = 1 if dbg == "attn1" else 8
    dbg_outs = {}
    if dbg is not None:
        for name, shape in (("d_r1", [KB, 128, TC]), ("d_y1", [KB, 128, TC]),
                            ("d_h", [TAU, 128, TC]), ("d_r2", [KB, 128, TC]),
                            ("d_zsgu", [32, 2 * INTER]), ("d_zsdn", [TAU, D]),
                            ("d_b01t", [128, D]), ("d_cs", [32, TC]),
                            ("d_hs", [TAU, TC])):
            dbg_outs[name] = nc.dram_tensor(name, shape, BF16,
                                            kind="ExternalOutput").ap()

    es = ExitStack()

    def mksem(name):
        return SemT(es.enter_context(nc.semaphore(name)))

    qXc = mksem("qXc"); qXs = mksem("qXs"); qXt = mksem("qXt")
    qMk = mksem("qMk"); qWc = mksem("qWc"); qWs = mksem("qWs")
    qBs = mksem("qBs"); qMt = mksem("qMt"); qWt = mksem("qWt")
    qRs = mksem("qRs"); qRl = mksem("qRl"); qY1 = mksem("qY1")
    qZL = mksem("qZL"); qZS = mksem("qZS"); qYL = mksem("qYL"); qZC = mksem("qZC")
    qQG = mksem("qQG"); qSB = mksem("qSB"); qHs = mksem("qHs")
    qHl = mksem("qHl"); qH2 = mksem("qH2"); qQD = mksem("qQD")
    qSD = mksem("qSD"); qYl = mksem("qYl"); qR2s = mksem("qR2s")
    qR2l = mksem("qR2l"); qOut = mksem("qOut"); qMisc = mksem("qMisc")
    qHlA = mksem("qHlA"); qHlB = mksem("qHlB")
    qHa = mksem("qHa"); qHb = mksem("qHb")
    qYa = mksem("qYa"); qYb = mksem("qYb")
    qRa = mksem("qRa"); qRb = mksem("qRb")
    qRlA = mksem("qRlA"); qRlB = mksem("qRlB")
    qXcA = mksem("qXcA"); qXcB = mksem("qXcB")
    qXsA = mksem("qXsA"); qXsB = mksem("qXsB")
    qWcA = mksem("qWcA"); qWcB = mksem("qWcB")
    qMtA = mksem("qMtA"); qMtB = mksem("qMtB")
    qRsA = mksem("qRsA"); qRsB = mksem("qRsB")
    qZL2 = mksem("qZL2")
    qQGA = mksem("qQGA"); qQGB = mksem("qQGB")
    qSBA = mksem("qSBA"); qSBB = mksem("qSBB")
    qZCA = mksem("qZCA"); qZCB = mksem("qZCB")
    qHsA = mksem("qHsA"); qHsB = mksem("qHsB")
    qSDA = mksem("qSDA"); qSDB = mksem("qSDB")
    qR2sS = [mksem(f"qR2s{j}") for j in range(4)]
    qOutA = mksem("qOutA"); qOutB = mksem("qOutB")
    vMr = mksem("vMr"); vB1 = mksem("vB1"); tB = mksem("tB")
    vWm = mksem("vWm"); tA = mksem("tA"); vAE = mksem("vAE")
    aSq = mksem("aSq"); tS1 = mksem("tS1"); vF1 = mksem("vF1")
    aF1 = mksem("aF1"); tBc = mksem("tBc"); vAp = mksem("vAp")
    tCs = mksem("tCs"); vCs = mksem("vCs"); vZB = mksem("vZB")
    vDQ = mksem("vDQ"); tMM = mksem("tMM"); aSI = mksem("aSI"); vEV = mksem("vEV")
    tHs = mksem("tHs"); vHs = mksem("vHs"); vZD = mksem("vZD")
    vDQ2 = mksem("vDQ2"); tDN = mksem("tDN"); vDE = mksem("vDE")
    tTf = mksem("tTf"); aTE = mksem("aTE")
    aSq2 = mksem("aSq2"); tS2 = mksem("tS2"); vF2 = mksem("vF2")
    aF2 = mksem("aF2"); tBc2 = mksem("tBc2"); vAp2 = mksem("vAp2")
    vSS = mksem("vSS"); qFl = mksem("qFl")

    with es:
        toep = es.enter_context(nc.sbuf_tensor("toep_s", [128, 173], BF16))
        onesr = es.enter_context(nc.sbuf_tensor("onesr_s", [1, 128], BF16))
        hsum = es.enter_context(nc.sbuf_tensor("hsum", [TAU, TC], BF16))
        row_m = es.enter_context(nc.sbuf_tensor("row_m", [1, TC], F32))
        row_a = es.enter_context(nc.sbuf_tensor("row_a", [1, TC], F32))
        row_b = es.enter_context(nc.sbuf_tensor("row_b", [1, TC], F32))
        row_c = es.enter_context(nc.sbuf_tensor("row_c", [1, TC], F32))
        row_rsb = es.enter_context(nc.sbuf_tensor("row_rsb", [1, TC], BF16))
        row_mrb = es.enter_context(nc.sbuf_tensor("row_mrb", [1, TC], BF16))
        fl_s = es.enter_context(nc.sbuf_tensor("fl_s", [1, 32], BF16))
        fl_v = es.enter_context(nc.sbuf_tensor("fl_v", [1, 32], BF16))
        fl_c = es.enter_context(nc.sbuf_tensor("fl_c", [1, 32], BF16))

        def vfl(ve, sem, k=1):
            ve.tensor_copy(fl_v[:], fl_s[:]).then_inc(sem.h, k)

        def afl(ac, sem, k=1):
            ac.activation(fl_c[:], fl_s[:], AF.Copy).then_inc(sem.h, k)
        csum = es.enter_context(nc.sbuf_tensor("csum", [32, TC], BF16))
        zsd = es.enter_context(nc.sbuf_tensor("zsd", [TAU, D], BF16))

        with nc.Block() as block:
            def c_gp(gp):
                gp.dma_start(out=toep[:], in_=toep_in).then_inc(qMisc.h, 16)
                gp.dma_start(out=onesr[:], in_=ones_in).then_inc(qMisc.h, 16)
                gp.dma_start(out=fl_s[:], in_=ones_in[:, 0:32]
                             ).then_inc(qFl.h, 16)
            block.gpsimd(c_gp)
            qMisc.n += 32

            esA = ExitStack()
            bsel = esA.enter_context(nc.sbuf_tensor("bsel_s", [128, 256], BF16))
            eye = esA.enter_context(nc.sbuf_tensor("eye_s", [128, 128], BF16))
            xT = esA.enter_context(nc.sbuf_tensor("xT", [128, KB * TC], BF16,
                                                  is_transpose_dest=True))
            xc = [esA.enter_context(nc.sbuf_tensor(f"xc{i}", [128, D], BF16))
                  for i in range(2)]
            wc = [esA.enter_context(nc.sbuf_tensor(f"wc{i}", [128, D], BF16))
                  for i in range(2)]
            mk = [esA.enter_context(nc.sbuf_tensor(f"mk{i}", [128, D], I32))
                  for i in range(1)]
            s1i = esA.enter_context(nc.sbuf_tensor("s1i", [128, 128], F32))
            s1b = esA.enter_context(nc.sbuf_tensor("s1b", [128, 128], BF16))
            b01 = esA.enter_context(nc.sbuf_tensor("b01", [128, 128], BF16))
            wcm = [esA.enter_context(nc.sbuf_tensor(f"wcm{i}", [128, D], BF16))
                   for i in range(2)]
            mkb = [esA.enter_context(nc.sbuf_tensor(f"mkb{i}", [128, 128], BF16))
                   for i in range(2)]
            wT = [esA.enter_context(nc.sbuf_tensor(f"wT{i}", [128, KB * 256], BF16))
                  for i in range(2)]
            rst = [esA.enter_context(nc.sbuf_tensor(f"rst{i}", [128, TC], BF16))
                   for i in range(2)]
            psA = [esA.enter_context(nc.psum_tensor(f"psA{i}", [128, TC], F32))
                   for i in range(2)]
            psT = [esA.enter_context(nc.psum_tensor(f"psT{i}", [128, 512], F32))
                   for i in range(2)]
            NCH2 = 2 * NCH
            NFILL = 8
            N_WT = (2 * NCH2) if do_attn else (32 if do_w else 0)

            def pa_gp(gp):
                gp.dma_start(out=bsel[:], in_=bsel_in).then_inc(qMisc.h, 16)
                gp.dma_start(out=eye[:], in_=eye_in).then_inc(qMisc.h, 16)
                for b8 in range(4 if do_x else 0):
                    gp.dma_start(
                        out=xT[:, TC * 8 * b8:TC * 8 * (b8 + 1)]
                        .rearrange("p (b t) -> p b t", b=8),
                        in_=xt_in[8 * b8:8 * (b8 + 1)].transpose([1, 0, 2])
                        ).then_inc(qXt.h, 16)
                for i in range(N_WT):
                    if i >= 2:
                        gp.wait_ge(vWm.h, i - 1)
                    s = qWcA if i % 2 == 0 else qWcB
                    gp.dma_start(out=wc[i % 2][:],
                                 in_=w_in[128 * i:128 * (i + 1), :]
                                 ).then_inc(s.h, 16)
                    s.n += 16
                    if i == 0:
                        gp.wait_ge(qBs.h, 16)
                    s = qMtA if i % 2 == 0 else qMtB
                    gp.dma_start(out=mkb[i % 2][:],
                                 in_=b01t_d[4 * i:4 * i + 4, 0:128]
                                 .unsqueeze(1).broadcast_to((4, 32, 128))
                                 ).then_inc(s.h, 16)
                    s.n += 16
            block.gpsimd(pa_gp)
            qMisc.n += 32

            def pa_sc(sc):
                sc.wait_ge(vB1.h, 4)
                sc.dma_start(out=b01t_d[0:128, 0:128], in_=b01[:]
                             ).then_inc(qBs.h, 16)
            block.scalar(pa_sc)
            qBs.n += 16

            def pa_sy(sy):
                for i in range(32):
                    if i >= 1:
                        sy.wait_ge(vMr.h, i)
                    sy.dma_start(out=mk[0][:],
                                 in_=m_in[128 * i:128 * (i + 1), :]
                                 ).then_inc(qMk.h, 16)
                for i in range(2 * NCH2 if do_attn else 0):
                    sy.wait_ge(vAE.h, i + 1)
                    s = qRsA if i % 2 == 0 else qRsB
                    sy.dma_start(out=r1_d[i], in_=rst[i % 2][:]
                                 ).then_inc(s.h, 16)
                    s.n += 16
            block.sync(pa_sy)
            qXt.n += 4 * 16 if do_x else 0
            qMk.n += 32 * 16

            def pa_ve(ve):
                ve.wait_ge(qFl.h, 16)
                for i in range(32):
                    ve.wait_ge(qMk.h, 16 * (i + 1))
                    src = mk[0][:].rearrange("p (k r) -> p k r", r=32)
                    ve.tensor_reduce(s1i[:], src, AX.X, ALU.add
                                     ).then_inc(vSS.h, 1)
                    vSS.n += 1
                    ve.wait_ge(vSS.h, vSS.n)
                    if i >= 1:
                        ve.wait_ge(tB.h, i)
                    ins_ = ve.tensor_copy(s1b[:], s1i[:])
                    ins_.then_inc(vMr.h, 1)
                    if i % 8 == 7:
                        q = i // 8
                        ve.wait_ge(tB.h, 8 * (q + 1))
                        ins_ = ve.tensor_single_scalar(b01[32 * q:32 * (q + 1), :],
                                                psA[1][0:32, 896:1024],
                                                0.0, ALU.is_gt)
                        ins_.then_inc(vB1.h, 1)
                def emit_tt(w):
                    ve.wait_ge((qWcA if w % 2 == 0 else qWcB).h,
                               16 * (w // 2 + 1))
                    ve.wait_ge((qMtA if w % 2 == 0 else qMtB).h,
                               16 * (w // 2 + 1))
                    if w >= 2:
                        ve.wait_ge(tTf.h, NFILL * (w - 1))
                    ins_ = ve.tensor_mul(
                        wcm[w % 2][:].rearrange("p (k r) -> p k r", r=32),
                        wc[w % 2][:].rearrange("p (k r) -> p k r", r=32),
                        mkb[w % 2][:].unsqueeze(2).broadcast_to((128, 128, 32)))
                    ins_.then_inc(vWm.h, 1)

                if do_attn:
                    for w0 in range(min(6, N_WT)):
                        emit_tt(w0)
                    for j in range(NCH2):
                        ve.wait_ge(tA.h, 32 * (j + 1))
                        for nt in range(2):
                            i = j * 2 + nt
                            if i >= 2:
                                ve.wait_ge((qRsA if i % 2 == 0 else qRsB).h,
                                           16 * ((i - 2) // 2 + 1))
                            ins_ = ve.tensor_add(rst[i % 2][:], psA[nt][:],
                                          xT[:, TC * i:TC * (i + 1)])
                            ins_.then_inc(vAE.h, 1)
                        for w2 in (2 * (j + 3), 2 * (j + 3) + 1):
                            if 6 <= w2 < N_WT:
                                emit_tt(w2)
                else:
                    for w in range(N_WT):
                        emit_tt(w)
            block.vector(pa_ve)
            vMr.n += 32
            vB1.n += 4
            vWm.n += N_WT
            vAE.n += 2 * NCH2 if do_attn else 0

            def pa_ac(ac):
                ac.wait_ge(qFl.h, 16)
                for f in range(NFILL * N_WT):
                    i = f // NFILL
                    fl2_ = f % NFILL
                    j = i // 2
                    t2 = i % 2
                    g4 = fl2_
                    ac.wait_ge(tTf.h, f + 1)
                    if do_attn and j >= 2:
                        ac.wait_ge(tA.h, 32 * (j - 1))
                    dst = wT[j % 2][:, 4096 * t2 + 512 * g4:
                                     4096 * t2 + 512 * (g4 + 1)]
                    ins_ = ac.activation(dst, psT[f % 2][:], AF.Copy)
                    ins_.then_inc(aTE.h, 1)
            block.scalar(pa_ac)
            aTE.n += NFILL * N_WT

            def pa_te(te):
                te.wait_ge(qMisc.h, 64)
                for i in range(32):
                    q, r = divmod(i, 8)
                    te.wait_ge(vMr.h, i + 1)
                    if r == 0 and q > 0:
                        te.wait_ge(vB1.h, q)
                    te.matmul(psA[1][0:32, 896:1024], bsel[:, 32 * r:32 * (r + 1)],
                              s1b[:], start=(r == 0), stop=(r == 7)
                              ).then_inc(tB.h, 1)
                def emit_fills(i):
                    te.wait_ge(vWm.h, i + 1)
                    for g4 in range(NFILL):
                        f = i * NFILL + g4
                        if f >= 2:
                            te.wait_ge(aTE.h, f - 1)
                        ins = None
                        for cc in range(4):
                            c = 4 * g4 + cc
                            ins = te.matmul(psT[f % 2][:, 128 * cc:128 * (cc + 1)],
                                            wcm[i % 2][:, 128 * c:128 * (c + 1)],
                                            eye[:], start=(cc == 0),
                                            stop=(cc == 3), skip_group_check=True)
                        ins.then_inc(tTf.h, 1)
                    te.then_last = None

                for i0 in range(min(4, N_WT)):
                    emit_fills(i0)
                if do_attn:
                    te.wait_ge(qXt.h, 4 * 16)
                    te.wait_ge(vB1.h, 4)
                for j in range(NCH2 if do_attn else 0):
                    if j > 0:
                        te.wait_ge(vAE.h, 2 * j)
                    te.wait_ge(aTE.h, NFILL * 2 * (j + 1))
                    for kb in range(KB):
                        ins = None
                        for nt in range(2):
                            for hf in range(2):
                                ins = te.matmul(
                                    psA[nt][:, 512 * hf:512 * (hf + 1)],
                                    wT[j % 2][:, 4096 * nt + 128 * kb:
                                              4096 * nt + 128 * (kb + 1)],
                                    xT[:, TC * kb + 512 * hf:
                                       TC * kb + 512 * (hf + 1)],
                                    start=(kb == 0), stop=(kb == KB - 1))
                        ins.then_inc(tA.h, 1)
                    for i2 in (2 * (j + 2), 2 * (j + 2) + 1):
                        if 4 <= i2 < N_WT:
                            emit_fills(i2)
                if not do_attn and do_w:
                    for i in range(4, N_WT):
                        emit_fills(i)
            block.tensor(pa_te)
            tB.n += 32
            tTf.n += NFILL * N_WT
            tA.n += 32 * NCH2 if do_attn else 0

            if dbg in ("mask", "xonly", "wprep", "attn", "attn1"):
                def dbg_gp(gp):
                    n = 0
                    if do_attn:
                        gp.wait_ge(qRsA.h, qRsA.n)
                        gp.wait_ge(qRsB.h, qRsB.n)
                        gp.dma_start(out=dbg_outs["d_r1"][0:4 * NCH],
                                     in_=r1_d[0:4 * NCH]
                                     ).then_inc(qMisc.h, 16)
                        n += 16
                    gp.wait_ge(qBs.h, qBs.n)
                    gp.dma_start(out=dbg_outs["d_b01t"][:], in_=b01t_d[:]
                                 ).then_inc(qMisc.h, 16)
                    n += 16
                    if do_x:
                        gp.wait_ge(qXt.h, qXt.n)
                        for b in range(KB):
                            gp.dma_start(out=dbg_outs["d_y1"][b],
                                         in_=xT[:, TC * b:TC * (b + 1)]
                                         ).then_inc(qMisc.h, 16)
                        n += 16 * KB
                    if do_w and not do_attn:
                        gp.wait_ge(aTE.h, aTE.n)
                        gp.dma_start(
                            out=dbg_outs["d_r2"][0:8].transpose([1, 0, 2]),
                            in_=wT[1][:, 0:8192].rearrange(
                                "p (b t) -> p b t", b=8)
                            ).then_inc(qMisc.h, 16)
                        n += 16
                    gp.wait_ge(qMisc.h, qMisc.n + n)
                block.gpsimd(dbg_gp)
                esA.close()
                return
            esA.close()

            esB = ExitStack()
            rsl = esB.enter_context(nc.sbuf_tensor("rsl", [128, KB * TC], BF16))
            y1T = esB.enter_context(nc.sbuf_tensor("y1T", [128, KB * TC], BF16))
            rbc = esB.enter_context(nc.sbuf_tensor("rbc", [128, TC], BF16))
            mrbc = esB.enter_context(nc.sbuf_tensor("mrbc", [128, TC], BF16))
            rsq = [esB.enter_context(nc.sbuf_tensor(f"rsq{i}", [128, TC], BF16))
                   for i in range(2)]
            tmpT = esB.enter_context(nc.sbuf_tensor("tmpT", [128, TC], BF16))
            psS = esB.enter_context(nc.psum_tensor("psS", [128, TC], F32))
            psR = esB.enter_context(nc.psum_tensor("psR", [128, TC], F32))
            psM = esB.enter_context(nc.psum_tensor("psM", [128, TC], F32))
            psC = esB.enter_context(nc.psum_tensor("psC", [128, TC], F32))

            def lb_sy(sy):
                sy.wait_ge(qRsA.h, qRsA.n)
                sy.wait_ge(qRsB.h, qRsB.n)
                sy.dma_start(
                    out=rsl[:, 0:16 * TC].rearrange("p (b t) -> p b t", b=16),
                    in_=r1_d[0:16].transpose([1, 0, 2])).then_inc(qRlA.h, 16)
                sy.dma_start(
                    out=rsl[:, 16 * TC:32 * TC].rearrange("p (b t) -> p b t",
                                                          b=16),
                    in_=r1_d[16:32].transpose([1, 0, 2])).then_inc(qRlB.h, 16)
            block.sync(lb_sy)
            qRlA.n += 16
            qRlB.n += 16

            def lb_ac(ac):
                for dt in range(KB):
                    ac.wait_ge((qRlA if dt < 16 else qRlB).h, 16)
                    if dt >= 2:
                        ac.wait_ge(tS1.h, dt - 1)
                    ins_ = ac.activation(rsq[dt % 2][:], rsl[:, TC * dt:TC * (dt + 1)],
                                  AF.Square)
                    ins_.then_inc(aSq.h, 1)
                ac.wait_ge(vF1.h, 4)
                ins_ = ac.activation(row_b[:], row_a[:], AF.Sqrt)
                ins_.then_inc(aF1.h, 1)
                ac.wait_ge(tBc.h, 4)
                ac.activation(rbc[:], psR[:], AF.Copy)
                ins_ = ac.activation(mrbc[:], psM[:], AF.Copy)
                ins_.then_inc(aF1.h, 2)
            block.scalar(lb_ac)
            aSq.n += KB
            aF1.n += 3

            def lb_te(te):
                for dt in range(KB):
                    te.wait_ge(aSq.h, dt + 1)
                    ins = None
                    for hf in range(2):
                        sl = slice(512 * hf, 512 * (hf + 1))
                        ins = te.matmul(
                            psS[0:64, sl], toep[:, 86:150],
                            rsl[:, TC * dt + 512 * hf:TC * dt + 512 * (hf + 1)],
                            start=(dt == 0), stop=False, skip_group_check=True)
                        ins = te.matmul(
                            psS[0:64, sl], toep[:, 54:118], rsq[dt % 2][:, sl],
                            start=False, stop=(dt == KB - 1), skip_group_check=True)
                    ins.then_inc(tS1.h, 1)
                te.wait_ge(vF1.h, 7)
                for hf in range(2):
                    sl = slice(512 * hf, 512 * (hf + 1))
                    te.matmul(psR[:, sl], onesr[:], row_rsb[:, sl], start=True,
                              stop=True).then_inc(tBc.h, 1)
                    te.matmul(psM[:, sl], onesr[:], row_mrb[:, sl], start=True,
                              stop=True).then_inc(tBc.h, 1)
                for dt in range(KB):
                    te.wait_ge(vAp.h, 2 * (dt + 1))
                    ins = None
                    for hf in range(2):
                        ins = te.matmul(
                            psC[0:32, 512 * hf:512 * (hf + 1)],
                            toep[:, 86 - dt:118 - dt],
                            y1T[:, TC * dt + 512 * hf:TC * dt + 512 * (hf + 1)],
                            start=(dt == 0), stop=(dt == KB - 1))
                    ins.then_inc(tCs.h, 1)
            block.tensor(lb_te)
            tS1.n += KB
            tBc.n += 4
            tCs.n += KB

            def lb_ve(ve):
                ve.wait_ge(tS1.h, KB)
                ve.tensor_scalar_mul(row_m[:], psS[0:1, :], 1.0 / D)
                ve.tensor_scalar_mul(row_a[:], psS[32:33, :], 1.0 / D)
                ve.tensor_mul(row_b[:], row_m[:], row_m[:])
                ve.tensor_sub(row_c[:], row_a[:], row_b[:])
                ins_ = ve.tensor_scalar_add(row_a[:], row_c[:], EPS)
                ins_.then_inc(vF1.h, 4)
                ve.wait_ge(aF1.h, 1)
                ve.reciprocal(row_c[:], row_b[:])
                ve.tensor_mul(row_a[:], row_m[:], row_c[:])
                ve.tensor_copy(row_rsb[:], row_c[:])
                ins_ = ve.tensor_copy(row_mrb[:], row_a[:])
                ins_.then_inc(vF1.h, 3)
                ve.wait_ge(aF1.h, 3)
                for dt in range(KB):
                    sl = slice(TC * dt, TC * (dt + 1))
                    ve.tensor_mul(tmpT[:], rsl[:, sl], rbc[:])
                    ins_ = ve.tensor_sub(y1T[:, sl], tmpT[:], mrbc[:])
                    ins_.then_inc(vAp.h, 2)
                ve.wait_ge(tCs.h, KB)
                ins_ = ve.tensor_copy(csum[:], psC[0:32, :])
                ins_.then_inc(vCs.h, 1)
            block.vector(lb_ve)
            vF1.n += 7
            vAp.n += 2 * KB
            vCs.n += 1

            def lb_sc(sc):
                for dt in range(KB):
                    sc.wait_ge(vAp.h, 2 * (dt + 1))
                    sc.dma_start(out=y1_d[dt], in_=y1T[:, TC * dt:TC * (dt + 1)]
                                 ).then_inc(qY1.h, 16)
            block.scalar(lb_sc)
            qY1.n += KB * 16

            if dbg == "ln1":
                def dbg_gp(gp):
                    gp.wait_ge(qY1.h, qY1.n)
                    gp.wait_ge(vCs.h, vCs.n)
                    gp.dma_start(out=dbg_outs["d_y1"][:], in_=y1_d[:]
                                 ).then_inc(qMisc.h, 16)
                    gp.dma_start(out=dbg_outs["d_cs"][:], in_=csum[:]
                                 ).then_inc(qMisc.h, 16)
                    gp.wait_ge(qMisc.h, qMisc.n + 32)
                block.gpsimd(dbg_gp)
                esB.close()
                return
            esB.close()

            esY = ExitStack()
            y1T2 = esY.enter_context(nc.sbuf_tensor("y1T2", [128, KB * TC], BF16))
            esZ = ExitStack()
            qzt = esZ.enter_context(nc.sbuf_tensor("qzt", [32, 688], I32))
            znb = esZ.enter_context(nc.sbuf_tensor("znb", [32, 5504], I32))
            sst = esZ.enter_context(nc.sbuf_tensor("sst", [32, 5504], BF16))
            zst = esZ.enter_context(nc.sbuf_tensor("zst", [32, 5504], BF16))
            qzt2 = esZ.enter_context(nc.sbuf_tensor("qzt2", [TAU, 512], I32))
            znb2 = esZ.enter_context(nc.sbuf_tensor("znb2", [TAU, D], I32))
            sst2 = esZ.enter_context(nc.sbuf_tensor("sst2", [TAU, D], BF16))

            def z_gp(gp):
                gp.wait_ge(qY1.h, qY1.n)
                gp.wait_ge(vCs.h, vCs.n)
                for ch in range(4):
                    if ch >= 1:
                        gp.wait_ge(vZB.h, 9 * ch)
                    gp.dma_start(out=qzt[:],
                                 in_=qzgu_in[:, 688 * ch:688 * (ch + 1)]
                                 ).then_inc(qZL.h, 16)
                    gp.dma_start(out=sst[:],
                                 in_=sgu_in[:, 5504 * ch:5504 * (ch + 1)]
                                 ).then_inc(qZL.h, 16)
                gp.dma_start(out=qzt2[:], in_=qzdn_in).then_inc(qZL2.h, 16)
                gp.dma_start(out=sst2[:], in_=sdn_in).then_inc(qZL2.h, 16)
                for dt in range(KB):
                    gp.dma_start(out=y1T2[:, TC * dt:TC * (dt + 1)], in_=y1_d[dt]
                                 ).then_inc(qYL.h, 16)
            block.gpsimd(z_gp)
            qZL.n += 8 * 16
            qZL2.n += 2 * 16
            qYL.n += KB * 16

            def z_ve(ve):
                for ch in range(4):
                    ve.wait_ge(qZL.h, 32 * (ch + 1))
                    if ch >= 1:
                        ve.wait_ge(qZS.h, 16 * ch)
                    for j in range(8):
                        ve.tensor_scalar(znb[:, 688 * j:688 * (j + 1)], qzt[:],
                                         4 * j, 15, ALU.logical_shift_right,
                                         ALU.bitwise_and).then_inc(vZB.h, 1)
                    nv = znb[:].rearrange("p (j q) -> p q j", j=8)
                    ins_ = ve.scalar_tensor_tensor(zst[:], nv, -1.0, sst[:], ALU.mult,
                                            ALU.mult)
                    ins_.then_inc(vZB.h, 1)
                ve.wait_ge(qZL.h, 128)
                ve.wait_ge(qZL2.h, 32)
                for j in range(8):
                    ve.tensor_scalar(znb2[:, 512 * j:512 * (j + 1)], qzt2[:],
                                     4 * j, 15, ALU.logical_shift_right,
                                     ALU.bitwise_and)
                nv2 = znb2[:].rearrange("p (j q) -> p q j", j=8)
                ins_ = ve.scalar_tensor_tensor(zsd[:], nv2, -1.0, sst2[:], ALU.mult,
                                        ALU.mult)
                ins_.then_inc(vZD.h, 1)
            block.vector(z_ve)
            vZB.n += 36
            vZD.n += 1

            def z_sc(sc):
                for ch in range(4):
                    sc.wait_ge(vZB.h, 9 * (ch + 1))
                    sc.dma_start(out=zsgu_d[:, 5504 * ch:5504 * (ch + 1)],
                                 in_=zst[:]).then_inc(qZS.h, 16)
            block.scalar(z_sc)
            qZS.n += 4 * 16
            esZ.close()

            esC = ExitStack()
            qg = [esC.enter_context(nc.sbuf_tensor(f"qg{i}", [128, 1024], I32))
                  for i in range(2)]
            nib = esC.enter_context(nc.sbuf_tensor("nib", [128, 2048], I32))
            deq = [esC.enter_context(nc.sbuf_tensor(f"deq{i}", [128, 8192], BF16))
                   for i in range(2)]
            sbt = [esC.enter_context(nc.sbuf_tensor(f"sbt{i}", [128, 8192], BF16))
                   for i in range(3)]
            zsg = [esC.enter_context(nc.sbuf_tensor(f"zsg{i}", [32, 1024], BF16))
                   for i in range(2)]
            hst = [esC.enter_context(nc.sbuf_tensor(f"hst{i}", [128, TC], BF16))
                   for i in range(2)]
            hso = [esC.enter_context(nc.sbuf_tensor(f"hso{i}", [128, TC], BF16))
                   for i in range(2)]
            psG = [esC.enter_context(nc.psum_tensor(f"psG{i}", [128, TC], F32))
                   for i in range(2)]
            psU = [esC.enter_context(nc.psum_tensor(f"psU{i}", [128, TC], F32))
                   for i in range(2)]

            def c_gp2(gp):
                gp.wait_ge(qZS.h, qZS.n)
                gp.wait_ge(qYL.h, qYL.n)
                gp.wait_ge(vZD.h, vZD.n)
                NZC4 = (TAU + 3) // 4
                for c in range(2):
                    t0 = 4 * c
                    tn = min(4, TAU - t0)
                    s = qZCA if c % 2 == 0 else qZCB
                    gp.dma_start(out=zsg[c % 2][:, 0:128 * tn],
                                 in_=zsgu_d[:, 128 * t0:128 * (t0 + tn)]
                                 ).then_inc(s.h, 16)
                    gp.dma_start(out=zsg[c % 2][:, 512:512 + 128 * tn],
                                 in_=zsgu_d[:, INTER + 128 * t0:
                                            INTER + 128 * (t0 + tn)]
                                 ).then_inc(s.h, 16)
                    s.n += 32
                for tau in range(TAU):
                    if tau % 4 == 0 and tau >= 4:
                        c = tau // 4 + 1
                        if c < NZC4:
                            t0 = 4 * c
                            tn = min(4, TAU - t0)
                            gp.wait_ge(tMM.h, 4 * (c - 1))
                            s = qZCA if c % 2 == 0 else qZCB
                            gp.dma_start(out=zsg[c % 2][:, 0:128 * tn],
                                         in_=zsgu_d[:, 128 * t0:128 * (t0 + tn)]
                                         ).then_inc(s.h, 16)
                            gp.dma_start(out=zsg[c % 2][:, 512:512 + 128 * tn],
                                         in_=zsgu_d[:, INTER + 128 * t0:
                                                    INTER + 128 * (t0 + tn)]
                                         ).then_inc(s.h, 16)
                            s.n += 32
                    if tau >= 3:
                        gp.wait_ge(vDQ.h, 2 * (tau - 2))
                    s = qSBA if tau % 2 == 0 else qSBB
                    for g in range(2):
                        src = sgu_in[:, INTER * g + 128 * tau:
                                     INTER * g + 128 * (tau + 1)]
                        src = src.unsqueeze(0).broadcast_to((128, 32, 128))
                        gp.dma_start(out=sbt[tau % 3][:, 4096 * g:4096 * (g + 1)],
                                     in_=src).then_inc(s.h, 16)
                        s.n += 16
                    if tau >= 2:
                        gp.wait_ge(vDQ.h, 2 * (tau - 1))
                    s = qQGA if tau % 2 == 0 else qQGB
                    gp.dma_start(out=qg[tau % 2][:],
                                 in_=qgu_in[tau].transpose([1, 0, 2])
                                 ).then_inc(s.h, 16)
                    s.n += 16
            block.gpsimd(c_gp2)

            def c_ve(ve):
                for tau in range(TAU):
                    par = tau % 2
                    if tau >= 2:
                        t2 = tau - 2
                        ve.wait_ge(aSI.h, t2 + 1)
                        if t2 >= 2:
                            ve.wait_ge((qHsA if t2 % 2 == 0 else qHsB).h,
                                       16 * ((t2 - 2) // 2 + 1))
                        ins_ = ve.tensor_mul(hso[t2 % 2][:], hst[t2 % 2][:],
                                      psU[t2 % 2][:])
                        ins_.then_inc(vEV.h, 1)
                    ve.wait_ge((qQGA if tau % 2 == 0 else qQGB).h,
                               16 * (tau // 2 + 1))
                    ve.wait_ge((qSBA if tau % 2 == 0 else qSBB).h,
                               32 * (tau // 2 + 1))
                    if tau >= 2:
                        ve.wait_ge(tMM.h, tau - 1)
                    for g in range(2):
                        for sub in range(2):
                            off = 512 * g + 256 * sub
                            for j in range(8):
                                ve.tensor_scalar(nib[:, 256 * j:256 * (j + 1)],
                                                 qg[par][:, off:off + 256],
                                                 4 * j, 15,
                                                 ALU.logical_shift_right,
                                                 ALU.bitwise_and)
                            nv = nib[:].rearrange("p (b kb a) -> p kb a b",
                                                  b=8, kb=16)
                            ins = ve.tensor_mul(
                                deq[par][:, 4096 * g + 2048 * sub:
                                         4096 * g + 2048 * (sub + 1)],
                                nv,
                                sbt[tau % 3][:, 4096 * g + 2048 * sub:
                                         4096 * g + 2048 * (sub + 1)])
                        ins.then_inc(vDQ.h, 1)
                for t2 in (TAU - 2, TAU - 1):
                    ve.wait_ge(aSI.h, t2 + 1)
                    ve.wait_ge((qHsA if t2 % 2 == 0 else qHsB).h,
                               16 * ((t2 - 2) // 2 + 1))
                    ins_ = ve.tensor_mul(hso[t2 % 2][:], hst[t2 % 2][:],
                                  psU[t2 % 2][:])
                    ins_.then_inc(vEV.h, 1)
            block.vector(c_ve)
            vDQ.n += 2 * TAU
            vEV.n += TAU

            def c_ac(ac):
                for tau in range(TAU):
                    ac.wait_ge(tMM.h, tau + 1)
                    if tau >= 2:
                        ac.wait_ge(vEV.h, tau - 1)
                    ins_ = ac.activation(hst[tau % 2][:], psG[tau % 2][:], AF.Silu)
                    ins_.then_inc(aSI.h, 1)
                    if tau >= 2:
                        t2 = tau - 2
                        ac.wait_ge(vEV.h, t2 + 1)
                        s = qHsA if t2 % 2 == 0 else qHsB
                        ac.dma_start(out=h_d[t2], in_=hso[t2 % 2][:]
                                     ).then_inc(s.h, 16)
                        s.n += 16
                for t2 in (TAU - 2, TAU - 1):
                    ac.wait_ge(vEV.h, t2 + 1)
                    s = qHsA if t2 % 2 == 0 else qHsB
                    ac.dma_start(out=h_d[t2], in_=hso[t2 % 2][:]
                                 ).then_inc(s.h, 16)
                    s.n += 16
            block.scalar(c_ac)
            aSI.n += TAU

            def c_te(te):
                te.wait_ge(vCs.h, vCs.n)
                for tau in range(TAU):
                    par = tau % 2
                    zch = tau // 4
                    zoff = 128 * (tau - 4 * zch)
                    te.wait_ge(vDQ.h, 2 * (tau + 1))
                    te.wait_ge((qZCA if zch % 2 == 0 else qZCB).h,
                               32 * (zch // 2 + 1))
                    if tau >= 2:
                        te.wait_ge(aSI.h, tau - 1)
                        te.wait_ge(vEV.h, tau - 1)
                    for g in range(2):
                        ps = psG[par] if g == 0 else psU[par]
                        ins = None
                        for hf in range(2):
                            sl = slice(512 * hf, 512 * (hf + 1))
                            for kb in range(KB):
                                ins = te.matmul(
                                    ps[:, sl],
                                    deq[par][:, 4096 * g + 128 * kb:
                                             4096 * g + 128 * (kb + 1)],
                                    y1T2[:, TC * kb + 512 * hf:
                                         TC * kb + 512 * (hf + 1)],
                                    start=(kb == 0), stop=False)
                            ins = te.matmul(
                                ps[:, sl],
                                zsg[zch % 2][:, 512 * g + zoff:
                                             512 * g + zoff + 128],
                                csum[:, sl], start=False, stop=True,
                                skip_group_check=True)
                        if g == 1:
                            ins.then_inc(tMM.h, 1)
            block.tensor(c_te)
            tMM.n += TAU

            if dbg == "gu":
                def dbg_gp(gp):
                    gp.wait_ge(qHsA.h, qHsA.n)
                    gp.wait_ge(qHsB.h, qHsB.n)
                    gp.dma_start(out=dbg_outs["d_h"][:], in_=h_d[:]
                                 ).then_inc(qMisc.h, 16)
                    gp.dma_start(out=dbg_outs["d_zsgu"][:], in_=zsgu_d[:]
                                 ).then_inc(qMisc.h, 16)
                    gp.wait_ge(qMisc.h, qMisc.n + 32)
                block.gpsimd(dbg_gp)
                esC.close()
                esY.close()
                return
            esC.close()
            esY.close()

            if dbg == "hsum":
                NGRUN = 0
            elif dbg is not None and dbg.startswith("dng"):
                NGRUN = int(dbg[3:])
            else:
                NGRUN = len(DN_GROUPS)
            DNG = DN_GROUPS[:NGRUN]
            esD = ExitStack()
            hsl = esD.enter_context(nc.sbuf_tensor("hsl", [128, NRES * TC], BF16))
            hs2 = [esD.enter_context(nc.sbuf_tensor(f"hs2{i}", [128, 8 * TC], BF16))
                   for i in range(2)]
            qds = esD.enter_context(nc.sbuf_tensor("qds", [128, TAU * 48], I32))
            nib2 = esD.enter_context(nc.sbuf_tensor("nib2", [128, 8 * 8 * 48], I32))
            dq2 = [esD.enter_context(nc.sbuf_tensor(f"dq2{i}", [128, 8 * 384], BF16))
                   for i in range(2)]
            sdt = [esD.enter_context(nc.sbuf_tensor(f"sdt{i}", [128, 8 * 384], BF16))
                   for i in range(2)]
            y1s = [esD.enter_context(nc.sbuf_tensor(f"y1s{i}", [128, TC], BF16))
                   for i in range(2)]
            r2s = [esD.enter_context(nc.sbuf_tensor(f"r2s{i}", [128, TC], BF16))
                   for i in range(4)]
            rsq2 = [esD.enter_context(nc.sbuf_tensor(f"rsq2{i}", [128, TC], BF16))
                    for i in range(2)]
            psD3 = [esD.enter_context(nc.psum_tensor(f"psD{i}", [128, TC], F32))
                    for i in range(3)]
            psS2 = esD.enter_context(nc.psum_tensor("psS2", [128, TC], F32))
            NSCH = (TAU - NRES + 7) // 8
            NG = len(DNG)
            evac_seq = [(gi, hl) for gi, nh in enumerate(DNG)
                        for hl in range(nh)]
            NEV = len(evac_seq)
            evac_cum = [0]
            for gi, nh in enumerate(DNG):
                evac_cum.append(evac_cum[-1] + nh)

            sdn_offs = {}
            _off = 0
            for _gi, _nh in enumerate(DN_GROUPS):
                _ewg = 8 * 16 * _nh
                for _c in range(NZCH):
                    _tn = min(8, TAU - 8 * _c)
                    sdn_offs[(_gi, _c)] = _off
                    _off += _tn * _ewg

            def d_gp(gp):
                gp.wait_ge(qHsA.h, qHsA.n)
                gp.wait_ge(qHsB.h, qHsB.n)
                for gi, nh in enumerate(DNG):
                    wg = 16 * nh
                    w0 = 48 * gi
                    for c in range(NZCH):
                        t0 = 8 * c
                        tn = min(8, TAU - t0)
                        idx = gi * NZCH + c
                        if idx >= 2:
                            gp.wait_ge(vDQ2.h, idx - 1)
                        off = sdn_offs[(gi, c)]
                        s = qSDA if idx % 2 == 0 else qSDB
                        src = sdnr_in[0, off:off + tn * 8 * wg]
                        src = src.unsqueeze(0).broadcast_to((128, tn * 8 * wg))
                        gp.dma_start(out=sdt[idx % 2][:, 0:tn * 8 * wg],
                                     in_=src).then_inc(s.h, 16)
                        s.n += 16
            block.gpsimd(d_gp)

            hsum_tgt = {}
            chunk_tgt = {}
            y1s_tgt = {}

            def d_sy(sy):
                sy.wait_ge(qHsA.h, qHsA.n)
                sy.wait_ge(qHsB.h, qHsB.n)
                sy.dma_start(
                    out=hsl[:, 0:18 * TC].rearrange("p (b t) -> p b t", b=18),
                    in_=h_d[0:18].transpose([1, 0, 2])).then_inc(qHlA.h, 16)
                sy.dma_start(
                    out=hsl[:, 18 * TC:36 * TC].rearrange("p (b t) -> p b t",
                                                          b=18),
                    in_=h_d[18:36].transpose([1, 0, 2])).then_inc(qHlB.h, 16)
                for tau in range(NRES, TAU):
                    if tau >= NRES + 2:
                        sy.wait_ge(tHs.h, tau - 1)
                    s = qHa if tau % 2 == 0 else qHb
                    sy.dma_start(out=hs2[tau % 2][:, 0:TC], in_=h_d[tau]
                                 ).then_inc(s.h, 16)
                    s.n += 16
                    hsum_tgt[tau] = (s, s.n)
                for gi, nh in enumerate(DNG):
                    wg = 16 * nh
                    w0g = TAU * 48 * gi
                    if gi >= 1:
                        sy.wait_ge(vDQ2.h, NZCH * gi)
                    sy.dma_start(out=qds[:, 0:TAU * wg],
                                 in_=qdn_in[:, w0g:w0g + TAU * wg]
                                 ).then_inc(qQD.h, 16)
                    for cp in range(NSCH):
                        t0 = NRES + 8 * cp
                        tn = min(8, TAU - t0)
                        if gi == 0:
                            sy.wait_ge(tHs.h, TAU)
                        if cp >= 2:
                            cons = 6 + (cp - 2) if cp - 2 <= 4 else NZCH
                            sy.wait_ge(tDN.h, NZCH * gi + cons)
                        elif gi >= 1:
                            sy.wait_ge(tDN.h, NZCH * gi)
                        s = qHa if cp % 2 == 0 else qHb
                        sy.dma_start(out=hs2[cp % 2][:, 0:tn * TC],
                                     in_=h_d[t0:t0 + tn].transpose([1, 0, 2])
                                     ).then_inc(s.h, 16)
                        s.n += 16
                        chunk_tgt[(gi, cp)] = (s, s.n)
                    for hl in range(nh):
                        k = evac_cum[gi] + hl
                        if k >= 2:
                            sy.wait_ge(vDE.h, k - 1)
                        ht = 3 * gi + hl
                        s = qYa if k % 2 == 0 else qYb
                        sy.dma_start(out=y1s[k % 2][:], in_=y1_d[ht]
                                     ).then_inc(s.h, 16)
                        s.n += 16
                        y1s_tgt[k] = (s, s.n)
            block.sync(d_sy)
            qHlA.n += 16
            qHlB.n += 16
            qQD.n += NG * 16

            def d_ve(ve):
                ve.wait_ge(tHs.h, TAU)
                ins_ = ve.tensor_copy(hsum[:], psS2[0:TAU, :])
                ins_.then_inc(vHs.h, 1)
                for gi, nh in enumerate(DNG):
                    wg = 16 * nh
                    ve.wait_ge(qQD.h, 16 * (gi + 1))
                    for c in range(NZCH):
                        t0 = 8 * c
                        tn = min(8, TAU - t0)
                        idx = gi * NZCH + c
                        ve.wait_ge((qSDA if idx % 2 == 0 else qSDB).h,
                                   16 * (idx // 2 + 1))
                        if c >= 2:
                            ve.wait_ge(tDN.h, NZCH * gi + c - 1)
                        elif gi >= 1:
                            ve.wait_ge(tDN.h, NZCH * gi - c)
                        for j in range(8):
                            ov = nib2[:, 0:tn * 8 * wg].rearrange(
                                "p (t j w) -> p t j w", t=tn, j=8)[:, :, j, :]
                            iv = qds[:, wg * t0:wg * (t0 + tn)].rearrange(
                                "p (t w) -> p t w", t=tn)
                            ve.tensor_scalar(ov, iv, 4 * j, 15,
                                             ALU.logical_shift_right,
                                             ALU.bitwise_and)
                        for tl in range(tn):
                            nv2 = nib2[:, 0:tn * 8 * wg].rearrange(
                                "p (t j w) -> p t w j", t=tn, j=8)[:, tl]
                            sv = sdt[idx % 2][:, 8 * wg * tl:8 * wg * (tl + 1)]
                            ins_ = ve.tensor_mul(dq2[c % 2][:, 8 * wg * tl:
                                                           8 * wg * (tl + 1)],
                                                 nv2, sv)
                        ins_.then_inc(vDQ2.h, 1)
                    ve.wait_ge(tDN.h, NZCH * (gi + 1))
                    for hl in range(nh):
                        k = evac_cum[gi] + hl
                        ys, yc = y1s_tgt[k]
                        ve.wait_ge(ys.h, yc)
                        if k >= 4:
                            ve.wait_ge(qR2sS[k % 4].h, 16 * ((k - 4) // 4 + 1))
                        ins_ = ve.tensor_add(r2s[k % 4][:], psD3[hl][:], y1s[k % 2][:])
                        ins_.then_inc(vDE.h, 1)
            block.vector(d_ve)
            vHs.n += 1
            vDQ2.n += NG * NZCH
            vDE.n += NEV

            def d_te(te):
                te.wait_ge(vZD.h, vZD.n)
                for tau in range(TAU):
                    if tau == 0:
                        te.wait_ge(qHlA.h, 16)
                    elif tau == 18:
                        te.wait_ge(qHlB.h, 16)
                    elif tau >= NRES:
                        hs_, hc_ = hsum_tgt[tau]
                        te.wait_ge(hs_.h, hc_)
                    src = (hsl[:, TC * tau:TC * (tau + 1)] if tau < NRES
                           else hs2[tau % 2][:, 0:TC])
                    ins = None
                    for hf in range(2):
                        ins = te.matmul(psS2[0:TAU, 512 * hf:512 * (hf + 1)],
                                        toep[:, 86 - tau:172 - tau],
                                        src[:, 512 * hf:512 * (hf + 1)],
                                        start=(tau == 0), stop=(tau == TAU - 1))
                    ins.then_inc(tHs.h, 1)
                te.wait_ge(vHs.h, vHs.n)
                stats_k = 0

                def emit_stats_upto(upto):
                    nonlocal stats_k
                    while stats_k < upto:
                        k = stats_k
                        te.wait_ge(aSq2.h, k + 1)
                        ins = None
                        for hf in range(2):
                            sl = slice(512 * hf, 512 * (hf + 1))
                            te.matmul(psS2[0:64, sl], toep[:, 86:150],
                                      r2s[k % 4][:, sl], start=(k == 0),
                                      stop=False, skip_group_check=True)
                            ins = te.matmul(psS2[0:64, sl], toep[:, 54:118],
                                            rsq2[k % 2][:, sl], start=False,
                                            stop=(k == NEV - 1),
                                            skip_group_check=True)
                        ins.then_inc(tS2.h, 1)
                        stats_k += 1

                for gi, nh in enumerate(DNG):
                    wg = 16 * nh
                    if gi >= 1:
                        te.wait_ge(vDE.h, evac_cum[gi])
                    for tau in range(TAU):
                        c = tau // 8
                        if tau % 8 == 0:
                            te.wait_ge(vDQ2.h, NZCH * gi + c + 1)
                        if tau >= NRES and (tau - NRES) % 8 == 0:
                            cp = (tau - NRES) // 8
                            ps_, pc_ = chunk_tgt[(gi, cp)]
                            te.wait_ge(ps_.h, pc_)
                        tl = tau - 8 * c
                        if tau < NRES:
                            mv = hsl[:, TC * tau:TC * (tau + 1)]
                        else:
                            tl2 = (tau - NRES) % 8
                            mv = hs2[((tau - NRES) // 8) % 2][
                                :, TC * tl2:TC * (tl2 + 1)]
                        ins = None
                        for hl in range(nh):
                            for hf in range(2):
                                ins = te.matmul(
                                    psD3[hl][:, 512 * hf:512 * (hf + 1)],
                                    dq2[c % 2][:, 8 * wg * tl + 128 * hl:
                                               8 * wg * tl + 128 * (hl + 1)],
                                    mv[:, 512 * hf:512 * (hf + 1)],
                                    start=(tau == 0), stop=False)
                        if tau % 8 == 7 or tau == TAU - 1:
                            if tau == TAU - 1:
                                for hl in range(nh):
                                    ht = 3 * gi + hl
                                    for hf in range(2):
                                        ins = te.matmul(
                                            psD3[hl][:, 512 * hf:512 * (hf + 1)],
                                            zsd[:, 128 * ht:128 * (ht + 1)],
                                            hsum[:, 512 * hf:512 * (hf + 1)],
                                            start=False, stop=True,
                                            skip_group_check=True)
                            ins.then_inc(tDN.h, 1)
                    if gi >= 1:
                        emit_stats_upto(evac_cum[gi])
                emit_stats_upto(NEV)
            block.tensor(d_te)
            tHs.n += TAU
            tDN.n += NG * NZCH
            tS2.n += NEV

            def d_ac(ac):
                for k in range(NEV):
                    ac.wait_ge(vDE.h, k + 1)
                    if k >= 2:
                        ac.wait_ge(tS2.h, k - 1)
                    ins_ = ac.activation(rsq2[k % 2][:], r2s[k % 4][:], AF.Square)
                    ins_.then_inc(aSq2.h, 1)
                    if k >= 2:
                        k2 = k - 2
                        gi2, hl2 = evac_seq[k2]
                        ht = 3 * gi2 + hl2
                        ac.wait_ge(tS2.h, k2 + 1)
                        s = qR2sS[k2 % 4]
                        ac.dma_start(out=r2_d[ht], in_=r2s[k2 % 4][:]
                                     ).then_inc(s.h, 16)
                        s.n += 16
                for k2 in range(max(0, NEV - 2), NEV):
                    gi2, hl2 = evac_seq[k2]
                    ht = 3 * gi2 + hl2
                    ac.wait_ge(tS2.h, k2 + 1)
                    s = qR2sS[k2 % 4]
                    ac.dma_start(out=r2_d[ht], in_=r2s[k2 % 4][:]
                                 ).then_inc(s.h, 16)
                    s.n += 16
            block.scalar(d_ac)
            aSq2.n += NEV


            def d_fin_ve(ve):
                ve.wait_ge(tS2.h, NEV)
                ve.tensor_scalar_mul(row_m[:], psS2[0:1, :], 1.0 / D)
                ve.tensor_scalar_mul(row_a[:], psS2[32:33, :], 1.0 / D)
                ve.tensor_mul(row_b[:], row_m[:], row_m[:])
                ve.tensor_sub(row_c[:], row_a[:], row_b[:])
                ins_ = ve.tensor_scalar_add(row_a[:], row_c[:], EPS)
                ins_.then_inc(vF2.h, 4)
                ve.wait_ge(aF2.h, 1)
                ve.reciprocal(row_c[:], row_b[:])
                ve.tensor_mul(row_a[:], row_m[:], row_c[:])
                ve.tensor_copy(row_rsb[:], row_c[:])
                ins_ = ve.tensor_copy(row_mrb[:], row_a[:])
                ins_.then_inc(vF2.h, 3)
            block.vector(d_fin_ve)
            vF2.n += 7

            def d_fin_ac(ac):
                ac.wait_ge(vF2.h, 4)
                ins_ = ac.activation(row_b[:], row_a[:], AF.Sqrt)
                ins_.then_inc(aF2.h, 1)
            block.scalar(d_fin_ac)
            aF2.n += 1

            if dbg is not None:
                def dbg_gp(gp):
                    for s in qR2sS:
                        gp.wait_ge(s.h, s.n)
                    gp.wait_ge(vHs.h, vHs.n)
                    gp.wait_ge(vZD.h, vZD.n)
                    n = 48
                    gp.dma_start(out=dbg_outs["d_r2"][:], in_=r2_d[:]
                                 ).then_inc(qMisc.h, 16)
                    gp.dma_start(out=dbg_outs["d_hs"][:], in_=hsum[:]
                                 ).then_inc(qMisc.h, 16)
                    gp.dma_start(out=dbg_outs["d_zsdn"][:], in_=zsd[:]
                                 ).then_inc(qMisc.h, 16)
                    if dbg == "hsum":
                        gp.dma_start(
                            out=dbg_outs["d_y1"][:].transpose([1, 0, 2]),
                            in_=hsl[:, 0:32 * TC].rearrange(
                                "p (b t) -> p b t", b=32)
                            ).then_inc(qMisc.h, 16)
                        gp.dma_start(
                            out=dbg_outs["d_r1"][0:4].transpose([1, 0, 2]),
                            in_=hsl[:, 32 * TC:36 * TC].rearrange(
                                "p (b t) -> p b t", b=4)
                            ).then_inc(qMisc.h, 16)
                        n += 32
                    gp.wait_ge(qMisc.h, qMisc.n + n)
                block.gpsimd(dbg_gp)
                esD.close()
                return
            esD.close()

            esE = ExitStack()
            r2l = [esE.enter_context(nc.sbuf_tensor(f"r2l{i}", [128, TC], BF16))
                   for i in range(2)]
            otile = [esE.enter_context(nc.sbuf_tensor(f"ot{i}", [128, TC], F32))
                     for i in range(2)]
            tmp2 = esE.enter_context(nc.sbuf_tensor("tmp2", [128, TC], BF16))
            rbc2 = esE.enter_context(nc.sbuf_tensor("rbc2", [128, TC], BF16))
            mrbc2 = esE.enter_context(nc.sbuf_tensor("mrbc2", [128, TC], BF16))
            psR2 = esE.enter_context(nc.psum_tensor("psR2", [128, TC], F32))
            psM2 = esE.enter_context(nc.psum_tensor("psM2", [128, TC], F32))

            def e_te(te):
                te.wait_ge(vF2.h, vF2.n)
                for hf in range(2):
                    sl = slice(512 * hf, 512 * (hf + 1))
                    te.matmul(psR2[:, sl], onesr[:], row_rsb[:, sl], start=True,
                              stop=True).then_inc(tBc2.h, 1)
                    te.matmul(psM2[:, sl], onesr[:], row_mrb[:, sl], start=True,
                              stop=True).then_inc(tBc2.h, 1)
            block.tensor(e_te)
            tBc2.n += 4

            def e_ac(ac):
                ac.wait_ge(tBc2.h, 4)
                ac.activation(rbc2[:], psR2[:], AF.Copy)
                ins_ = ac.activation(mrbc2[:], psM2[:], AF.Copy)
                ins_.then_inc(aF2.h, 2)
            block.scalar(e_ac)
            aF2.n += 2

            r2l_tgt = {}

            def e_sy(sy):
                for s in qR2sS:
                    sy.wait_ge(s.h, s.n)
                for dt in range(KB):
                    if dt >= 2:
                        sy.wait_ge(vAp2.h, 2 * (dt - 1))
                    s = qRa if dt % 2 == 0 else qRb
                    sy.dma_start(out=r2l[dt % 2][:], in_=r2_d[dt]
                                 ).then_inc(s.h, 16)
                    s.n += 16
                    r2l_tgt[dt] = (s, s.n)
            block.sync(e_sy)

            def e_ve(ve):
                ve.wait_ge(aF2.h, aF2.n)
                for dt in range(KB):
                    rs_, rc_ = r2l_tgt[dt]
                    ve.wait_ge(rs_.h, rc_)
                    if dt >= 2:
                        ve.wait_ge((qOutA if dt % 2 == 0 else qOutB).h,
                                   16 * ((dt - 2) // 2 + 1))
                    ve.tensor_mul(tmp2[:], r2l[dt % 2][:], rbc2[:])
                    ins_ = ve.tensor_sub(otile[dt % 2][:], tmp2[:], mrbc2[:])
                    ins_.then_inc(vAp2.h, 2)
            block.vector(e_ve)
            vAp2.n += 2 * KB

            def e_sc(sc):
                for dt in range(KB):
                    sc.wait_ge(vAp2.h, 2 * (dt + 1))
                    s = qOutA if dt % 2 == 0 else qOutB
                    sc.dma_start(out=out_t[128 * dt:128 * (dt + 1), :],
                                 in_=otile[dt % 2][:]).then_inc(s.h, 16)
                    s.n += 16
            block.scalar(e_sc)

            def e_gp(gp):
                gp.wait_ge(qOutA.h, qOutA.n)
                gp.wait_ge(qOutB.h, qOutB.n)
            block.gpsimd(e_gp)
            esE.close()
    return nc



def _consts():
    toep = np.zeros((128, 173), dtype=ml_dtypes.bfloat16)
    toep[:, 86] = 1.0
    bsel = np.zeros((128, 8, 32), dtype=ml_dtypes.bfloat16)
    for r in range(8):
        for p in range(128):
            bsel[p, r, 4 * r + p // 32] = 1.0
    bsel = bsel.reshape(128, 256)
    onesr = np.ones((1, 128), dtype=ml_dtypes.bfloat16)
    eye = np.eye(128, dtype=np.float32).astype(ml_dtypes.bfloat16)
    return toep, bsel, onesr, eye


def _prep_inputs(L0_attn_data, L0_attn_weight, L0_attn_mask, L0_ffn_qweight_gu,
                 L0_ffn_qzeros_gu, L0_ffn_scales_gu, L0_ffn_qweight_dn,
                 L0_ffn_qzeros_dn, L0_ffn_scales_dn):
    x = np.ascontiguousarray(
        np.asarray(L0_attn_data, dtype=np.float32).reshape(8192, D))
    w = np.ascontiguousarray(np.asarray(L0_attn_weight, dtype=np.float32))
    m = np.ascontiguousarray(np.asarray(L0_attn_mask, dtype=np.int32))
    qgu = np.asarray(L0_ffn_qweight_gu, dtype=np.int32)
    qgu_r = np.ascontiguousarray(
        qgu.reshape(32, 128, 2, 86, 16).transpose(3, 2, 1, 0, 4
                                                  ).reshape(86, 2, 128, 512))
    qzgu = np.ascontiguousarray(np.asarray(L0_ffn_qzeros_gu, dtype=np.int32))
    sgu = np.ascontiguousarray(np.asarray(L0_ffn_scales_gu, dtype=np.float32))
    qdn = np.asarray(L0_ffn_qweight_dn, dtype=np.int32)
    A = qdn.reshape(86, 128, 512).transpose(1, 0, 2)
    chunks = [A[:, :, 48 * g:48 * g + 48] for g in range(10)] + [A[:, :, 480:512]]
    qdn_r = np.ascontiguousarray(
        np.concatenate([c.reshape(128, -1) for c in chunks], axis=1))
    qzdn = np.ascontiguousarray(np.asarray(L0_ffn_qzeros_dn, dtype=np.int32))
    sdn = np.ascontiguousarray(np.asarray(L0_ffn_scales_dn, dtype=np.float32))
    sgu_r = np.ascontiguousarray(
        sgu.reshape(32, 2, TAU, 128).transpose(2, 1, 0, 3).reshape(TAU, 8192))
    parts = []
    for gi, nh in enumerate(DN_GROUPS):
        ewg = 8 * 16 * nh
        col0 = 384 * gi
        for c in range(NZCH):
            t0 = 8 * c
            tn = min(8, TAU - t0)
            parts.append(sdn[t0:t0 + tn, col0:col0 + ewg].reshape(-1))
    sdn_r = np.ascontiguousarray(np.concatenate(parts))[None, :]
    toep, bselc, onesr, eye = _consts()
    shared = dict(w=w, m=m, qgu=qgu_r, qzgu=qzgu, sgu=sgu, qdn=qdn_r,
                  qzdn=qzdn, sdn=sdn, sgur=sgu_r, sdnr=sdn_r,
                  toep=np.asarray(toep),
                  bsel=np.asarray(bselc), onesr=np.asarray(onesr),
                  eye=np.asarray(eye))
    in_maps = []
    for c in range(NCORE):
        im = dict(shared)
        im["xt"] = np.ascontiguousarray(
            x[TC * c:TC * (c + 1)].T).reshape(KB, 128, TC)
        in_maps.append(im)
    return in_maps


def kernel(**inputs):
    in_maps = _prep_inputs(**inputs)
    nc = bass.Bass("TRN2", target_bir_lowering=False)
    build(nc)
    res = run_bass_kernel_spmd(nc, in_maps, core_ids=list(range(NCORE)))
    out = np.concatenate([r["out_t"] for r in res.results], axis=1)
    return np.ascontiguousarray(out.T).reshape(2, 4096, 4096).astype(np.float32)
